# version 07
# speedup vs baseline: 1.0140x; 1.0115x over previous
import sys
import numpy as np
from contextlib import ExitStack

try:
    import concourse.bass as bass
except ImportError:
    sys.path.insert(0, "/opt/trn_rl_repo")
    import concourse.bass as bass
import concourse.mybir as mybir
from concourse.bass_utils import run_bass_kernel_spmd

F32 = mybir.dt.float32
BF16 = mybir.dt.bfloat16
AF = mybir.ActivationFunctionType
ALU = mybir.AluOpType
AX = mybir.AxisListType

S = 4096
D = 1024
NCH = 8
TT = 512
NTT = S // TT
LN_EPS = 1e-5

ENGINES = ("pe", "act", "dve", "pool", "sp")
_ENG_ATTR = {"pe": "tensor", "act": "scalar", "dve": "vector",
             "pool": "gpsimd", "sp": "sync"}


class Prog:

    def __init__(self, nc):
        self.nc = nc
        self.es = ExitStack()
        self.steps = []
        self.cnt = {}
        self._rd = {}
        for e in ENGINES:
            self.cnt["c." + e] = 0
            self.cnt["d." + e] = 0

    def rd(self, key, *toks):
        self._rd.setdefault(key, []).extend(toks)

    def wr(self, key):
        w = self._rd.get(key, [])
        self._rd[key] = []
        return list(w)

    def _tgt(self):
        sc = getattr(self, "_scopes", None)
        return sc[-1] if sc else self.es

    def push_scope(self):
        if not hasattr(self, "_scopes"):
            self._scopes = []
        self._scopes.append(ExitStack())

    def pop_scope(self):
        self._scopes.pop().close()

    def sb(self, name, shape, dt):
        return self._tgt().enter_context(self.nc.sbuf_tensor(name, shape, dt))

    def psum(self, name, shape, dt=F32):
        return self._tgt().enter_context(self.nc.psum_tensor(name, shape, dt))

    def step(self, e, fn, waits=()):
        self.cnt["c." + e] += 1
        tok = ("c." + e, self.cnt["c." + e])
        self.steps.append((e, "c", tuple(waits), fn))
        return tok

    def dma(self, e, fn, waits=(), n=1):
        self.cnt["d." + e] += 16 * n
        tok = ("d." + e, self.cnt["d." + e])
        self.steps.append((e, "d", tuple(waits), fn, n))
        return tok

    def barrier(self):
        toks = [(s, v) for s, v in self.cnt.items() if v > 0]
        return [self.step(e, None, waits=toks) for e in ENGINES]

    def _dma_run_ends(self):
        ends = {}
        for e in ENGINES:
            key = "d." + e
            cum = 0
            run = []
            m = {}
            def flush():
                for v in run:
                    m[v] = run[-1] if run else v
                del run[:]
            for (_e, kind, waits, fn, n) in (
                    (s[0], s[1], s[2], s[3], s[4]) for s in self._steps5()
                    if s[0] == e):
                if kind != "d":
                    flush()
                    continue
                if waits:
                    flush()
                cum += 16 * n
                run.append(cum)
            flush()
            ends[key] = m
        return ends

    def _steps5(self):
        for st in self.steps:
            yield (st[0], st[1], st[2], st[3], st[4] if len(st) > 4 else 1)

    def emit(self):
        nc = self.nc
        run_ends = self._dma_run_ends()
        with ExitStack() as es2:
            sems = {}
            for e in ENGINES:
                sems["c." + e] = es2.enter_context(nc.semaphore("sc_" + e))
                sems["d." + e] = es2.enter_context(nc.semaphore("sd_" + e))
            with nc.Block() as blk:
                for e in ENGINES:
                    mysteps = [st for st in self.steps if st[0] == e]

                    def body(eng, mysteps=mysteps, e=e):
                        csem = sems["c." + e]
                        dsem = sems["d." + e]
                        cname = "c." + e
                        wm0 = {}
                        absorb = []
                        for st in mysteps:
                            (_ee, kind0, waits0, _f) = st[:4]
                            emitted = 0
                            dr = 0
                            for (s, v) in waits0:
                                if s == cname:
                                    dr += 1
                                    continue
                                if s.startswith("d."):
                                    v = run_ends[s].get(v, v)
                                if wm0.get(s, 0) >= v:
                                    continue
                                wm0[s] = v
                                emitted += 1
                            absorb.append(kind0 == "c" and emitted == 0
                                          and dr == 0)
                        watermark = {}
                        pend = 0
                        for idx, st in enumerate(mysteps):
                            (_e, kind, waits, fn) = st[:4]
                            if pend and not absorb[idx]:
                                eng.drain()
                                eng.sem_inc(csem, pend)
                                pend = 0
                            need_drain = False
                            for (s, v) in waits:
                                if s == cname:
                                    need_drain = True
                                    continue
                                if s.startswith("d."):
                                    v = run_ends[s].get(v, v)
                                if watermark.get(s, 0) >= v:
                                    continue
                                watermark[s] = v
                                eng.wait_ge(sems[s], v)
                            if need_drain:
                                eng.drain()
                            insts = fn(eng) if fn is not None else None
                            if kind == "c":
                                pend += 1
                            else:
                                if insts is None:
                                    insts = []
                                if not isinstance(insts, (list, tuple)):
                                    insts = [insts]
                                for ins in insts:
                                    ins.then_inc(dsem, 16)
                        if pend:
                            eng.drain()
                            eng.sem_inc(csem, pend)

                    getattr(blk, _ENG_ATTR[e])(body)
            self.es.close()


class Rot:

    def __init__(self, tiles):
        self.tiles = tiles
        self.i = 0
        self.readers = [[] for _ in tiles]

    def acquire(self):
        i = self.i % len(self.tiles)
        self.i += 1
        w = self.readers[i]
        self.readers[i] = []
        return self.tiles[i], w, i

    def read(self, i, *toks):
        self.readers[i].extend(toks)


class Common:
    def __init__(self, p, aps):
        self.p = p
        self.aps = aps
        self.ones1 = p.sb("ones1", [1, 128], F32)
        self.ones1bf = p.sb("ones1bf", [1, 128], BF16)
        self.onescol_bf = p.sb("onescol_bf", [128, 1], BF16)
        self.ident = p.sb("ident128", [128, 128], F32)
        self.identbf = p.sb("identbf", [128, 128], BF16)
        t1 = p.dma("sp", lambda e: e.dma_start(out=self.ones1[:], in_=aps["ones1"]))
        t2 = p.dma("pool", lambda e: e.dma_start(out=self.onescol_bf[:], in_=aps["onescol"]))
        t3 = p.dma("sp", lambda e: e.dma_start(out=self.ident[:], in_=aps["ident128"]))
        t4 = p.dma("pool", lambda e: e.dma_start(out=self.ones1bf[:], in_=aps["ones1"]))
        t5 = p.dma("pool", lambda e: e.dma_start(out=self.identbf[:], in_=aps["ident128"]))
        self.ready = [t1, t2, t3, t4, t5]
        self.xt = Rot([p.sb("ln_xt0", [128, NCH, TT], F32)])
        self.tmp = Rot([p.sb("ln_tmp0", [128, NCH, TT], F32)])
        self.xsq = p.sb("ln_xsq", [128, NCH, TT], BF16)
        self.xbf = p.sb("ln_xbf", [128, NCH, TT], BF16)
        self._rows = p.sb("ln_rows", [1, 3 * TT + 1], F32)
        self.rowA = self._rows[:, 0:TT]
        self.rowB = self._rows[:, TT:2 * TT]
        self.rowC = self._rows[:, 2 * TT:3 * TT]
        self.ps_stat = p.psum("ln_ps_stat", [1, 2 * TT])
        self.ps_bc = p.psum("ln_ps_bc", [128, TT])
        self.rbc_sb = p.sb("ln_rbc", [128, TT], F32)
        self.mbc_sb = p.sb("ln_mbc", [128, TT], F32)
        self.eps = self._rows[:, 3 * TT:3 * TT + 1]
        self.ready.append(p.step("dve", lambda e: e.memset(self.eps, LN_EPS)))

    def ln_rows(self, p, src_bf, src_sq, nchunk, dim, extra_waits):
        def mm(e):
            for c in range(nchunk):
                e.matmul(self.ps_stat[:, 0:TT], self.onescol_bf[:],
                         src_bf[:, c, :], start=(c == 0), stop=(c == nchunk - 1))
            for c in range(nchunk):
                e.matmul(self.ps_stat[:, TT:2 * TT], self.onescol_bf[:],
                         src_sq[:, c, :], start=(c == 0), stop=(c == nchunk - 1))
        t_mm = p.step("pe", mm,
                      waits=list(extra_waits) + self.ready + p.wr("ln_stat")
                      + p.wr("ln_rows"))
        p.rd("ln_src", t_mm)
        p.rd("ln_src2", t_mm)
        inv = 1.0 / dim

        def rows(e):
            e.tensor_scalar_mul(self.rowA, self.ps_stat[:, 0:TT], inv)
            e.tensor_scalar_mul(self.rowB, self.ps_stat[:, TT:2 * TT], inv)
            e.tensor_mul(self.rowC, self.rowA, self.rowA)
            e.tensor_sub(self.rowB, self.rowB, self.rowC)
        t_rows = p.step("dve", rows, waits=[t_mm])
        p.rd("ln_stat", t_rows)
        t_sq = p.step("act", lambda e: e.activation(
            out=self.rowB, in_=self.rowB, func=AF.Sqrt,
            bias=self.eps), waits=[t_rows])

        def recip(e):
            e.reciprocal(self.rowB, self.rowB)
            e.tensor_mul(self.rowA, self.rowA, self.rowB)
        t_rec = p.step("dve", recip, waits=[t_sq])

        t_bc1 = p.step("pe", lambda e: e.matmul(
            self.ps_bc[:], self.ones1[:], self.rowB, start=True, stop=True),
            waits=[t_rec] + p.wr("ln_bc"))
        t_evr = p.step("act", lambda e: e.copy(
            out=self.rbc_sb[:], in_=self.ps_bc[:]),
            waits=[t_bc1] + p.wr("ln_rbc"))
        t_bc2 = p.step("pe", lambda e: e.matmul(
            self.ps_bc[:], self.ones1[:], self.rowA, start=True, stop=True),
            waits=[t_rec, t_evr])
        p.rd("ln_rows", t_bc2)
        t_evm = p.step("act", lambda e: e.copy(
            out=self.mbc_sb[:], in_=self.ps_bc[:]),
            waits=[t_bc2] + p.wr("ln_mbc"))
        p.rd("ln_bc", t_evm)
        return t_evr, t_evm


def x_ap(x_dram, t0, tlen):
    return x_dram.rearrange("(c p) t -> p c t", p=128)[:, :, t0:t0 + tlen]


def emit_ln_core(p, cm, src, t_src, out_bf=None, out_f32=None, out_waits=(),
                 gain_col=None, bias_col=None, gb_waits=()):
    def casts(e):
        for c in range(NCH):
            e.tensor_copy(cm.xbf[:, c, :], src[:, c, :])
    t_cast = p.step("pool", casts, waits=list(t_src) + p.wr("ln_src"))

    def squares(e):
        for c in range(NCH):
            e.activation(out=cm.xsq[:, c, :], in_=src[:, c, :], func=AF.Square)
    t_sqr = p.step("act", squares, waits=list(t_src) + p.wr("ln_src2"))

    t_bc, t_evm = cm.ln_rows(p, cm.xbf, cm.xsq, NCH, D, [t_cast, t_sqr])

    tmp, tw, ti = cm.tmp.acquire()

    def mul(e):
        for c in range(NCH):
            e.tensor_mul(tmp[:, c, :], src[:, c, :], cm.rbc_sb[:])
    t_mul = p.step("dve", mul, waits=[t_bc] + list(t_src) + tw)
    p.rd("ln_rbc", t_mul)

    outs = []
    if gain_col is None:
        tgt = out_bf if out_bf is not None else out_f32

        def sub(e):
            for c in range(NCH):
                e.tensor_sub(tgt[:, c, :], tmp[:, c, :], cm.mbc_sb[:])
        t_out = p.step("pool", sub, waits=[t_mul, t_evm] + list(out_waits))
        p.rd("ln_mbc", t_out)
        cm.tmp.read(ti, t_out)
        t_last = t_out
        if out_bf is not None and out_f32 is not None:
            def cp2(e):
                for c in range(NCH):
                    e.tensor_copy(out_f32[:, c, :], out_bf[:, c, :])
            t_last = p.step("dve", cp2, waits=[t_out])
    else:
        def sub(e):
            for c in range(NCH):
                e.tensor_sub(tmp[:, c, :], tmp[:, c, :], cm.mbc_sb[:])
        t_s = p.step("pool", sub, waits=[t_mul, t_evm])
        p.rd("ln_mbc", t_s)
        tgt = out_f32 if out_f32 is not None else out_bf

        def aff(e):
            for c in range(NCH):
                e.tensor_scalar(out=tgt[:, c, :], in0=tmp[:, c, :],
                                scalar1=gain_col[:, c:c + 1],
                                scalar2=bias_col[:, c:c + 1],
                                op0=ALU.mult, op1=ALU.add)
        t_out = p.step("dve", aff,
                       waits=[t_s] + list(out_waits) + list(gb_waits))
        cm.tmp.read(ti, t_out)
        t_last = t_out
        if out_bf is not None and out_f32 is not None:
            def cp2(e):
                for c in range(NCH):
                    e.tensor_copy(out_bf[:, c, :], out_f32[:, c, :])
            t_last = p.step("pool", cp2, waits=[t_out])
    return t_out, t_last


def emit_ln_in_tile(p, cm, x_dram, tt, out_bf, out_waits, gain_col=None,
                    bias_col=None, gb_waits=(), out_f32=None):
    xt, xw, xi = cm.xt.acquire()
    t_ld = p.dma("sp", lambda e: e.dma_start(
        out=xt[:], in_=x_ap(x_dram, tt * TT, TT)), waits=xw + cm.ready)
    t_out, t_last = emit_ln_core(p, cm, xt, [t_ld], out_bf=out_bf,
                                 out_f32=out_f32, out_waits=out_waits,
                                 gain_col=gain_col, bias_col=bias_col,
                                 gb_waits=gb_waits)
    cm.xt.read(xi, t_last)
    return t_out, t_last


def emit_resid_out_tile(p, cm, y_sb, t_y, x_dram_in, x_dram_out, tt):
    xt, xw, xi = cm.xt.acquire()
    t_ld = p.dma("sp", lambda e: e.dma_start(
        out=xt[:], in_=x_ap(x_dram_in, tt * TT, TT)), waits=xw + cm.ready)

    def casts(e):
        for c in range(NCH):
            e.tensor_copy(cm.xbf[:, c, :], y_sb[:, c, :])
    t_cast = p.step("pool", casts, waits=[t_y] + p.wr("ln_src"))

    def squares(e):
        for c in range(NCH):
            e.activation(out=cm.xsq[:, c, :], in_=y_sb[:, c, :], func=AF.Square)
    t_sqr = p.step("act", squares, waits=[t_y] + p.wr("ln_src2"))

    t_bc, t_evm = cm.ln_rows(p, cm.xbf, cm.xsq, NCH, D, [t_cast, t_sqr])

    tmp, tw, ti = cm.tmp.acquire()

    def mul(e):
        for c in range(NCH):
            e.tensor_mul(tmp[:, c, :], y_sb[:, c, :], cm.rbc_sb[:])
    t_mul = p.step("dve", mul, waits=[t_bc, t_y] + tw)
    p.rd("ln_rbc", t_mul)

    def addx(e):
        for c in range(NCH):
            e.tensor_add(tmp[:, c, :], tmp[:, c, :], xt[:, c, :])
    t_add = p.step("pool", addx, waits=[t_mul, t_ld])

    def subm(e):
        for c in range(NCH):
            e.tensor_sub(tmp[:, c, :], tmp[:, c, :], cm.mbc_sb[:])
    t_sub = p.step("dve", subm, waits=[t_add, t_evm])
    p.rd("ln_mbc", t_sub)

    t_st = p.dma("act", lambda e: e.dma_start(
        out=x_ap(x_dram_out, tt * TT, TT), in_=tmp[:]), waits=[t_sub])
    cm.xt.read(xi, t_add)
    cm.tmp.read(ti, t_st)
    return t_st, [t_mul, t_cast, t_sqr]


def emit_L0_ffn(p, cm, aps, x_in, x_out, dbg=None):
    sb, psum = p.sb, p.psum
    base_w = sb("l0f_basew", [128, NCH, D], BF16)
    A_all = sb("l0f_A", [128, NCH, 128], BF16)
    B_stk = sb("l0f_B", [128, D], BF16)
    rw = sb("l0f_rw", [128, NCH, 8], BF16)
    base_b = sb("l0f_bb", [128, NCH], F32)
    rb_col = sb("l0f_rb", [8, 1], F32)
    rep8 = sb("l0f_rep8", [8, 128], F32)
    ident8 = sb("l0f_id8", [8, 8], F32)

    tw = [
        p.dma("pool", lambda e: e.dma_start(
            out=base_w[:], in_=aps["L0f_basewT"].rearrange("(c p) o -> p c o", p=128))),
        p.dma("pool", lambda e: e.dma_start(
            out=A_all[:], in_=aps["L0f_A"].rearrange("(c p) r -> p c r", p=128))),
        p.dma("pool", lambda e: e.dma_start(out=B_stk[:], in_=aps["L0f_B"])),
        p.dma("pool", lambda e: e.dma_start(
            out=rw[:], in_=aps["L0f_rwT"].rearrange("(c p) e -> p c e", p=128))),
        p.dma("sp", lambda e: e.dma_start(out=base_b[:], in_=aps["L0f_baseb"])),
        p.dma("sp", lambda e: e.dma_start(out=rb_col[:], in_=aps["L0f_rb"])),
        p.dma("sp", lambda e: e.dma_start(out=rep8[:], in_=aps["rep8"])),
        p.dma("sp", lambda e: e.dma_start(out=ident8[:], in_=aps["ident8"])),
    ]

    xln = Rot([sb(f"l0f_xln{i}", [128, NCH, TT], BF16) for i in range(2)])
    ytile = Rot([sb(f"l0f_y{i}", [128, NCH, TT], F32) for i in range(2)])
    rlog = sb("l0f_rlog", [8, TT], F32)
    ltok = sb("l0f_ltok", [128, 4, 8], F32)
    mtok = sb("l0f_mtok", [128, 4, 8], F32)
    wtok = sb("l0f_wtok", [128, 4, 8], F32)
    col = sb("l0f_col", [128, 8], F32)
    wrow = sb("l0f_wrow", [8, TT], F32)
    ax = sb("l0f_ax", [128, TT], F32)
    wax = sb("l0f_wax", [128, TT], BF16)

    ps_r = psum("l0f_psr", [8, TT])
    ps_ax = psum("l0f_psax", [128, TT])
    ps_y = [psum(f"l0f_psy{i}", [128, TT]) for i in range(2)]
    ps_t = ps_ax
    ps_w16 = ps_y[1]

    for tt in range(NTT):
        xl, xlw, xli = xln.acquire()
        t_xln, _ = emit_ln_in_tile(p, cm, x_in, tt, xl, xlw)

        def mm_r(e, xl=xl):
            for c in range(NCH):
                e.matmul(ps_r[:], rw[:, c, :], xl[:, c, :],
                         start=(c == 0), stop=(c == NCH - 1))
        t_r = p.step("pe", mm_r, waits=[t_xln] + tw + p.wr("l0f_psr"))
        t_evr = p.step("act", lambda e: e.activation(
            out=rlog[:], in_=ps_r[:], func=AF.Identity, bias=rb_col[:, 0:1]),
            waits=[t_r] + p.wr("l0f_rlog"))
        p.rd("l0f_psr", t_evr)

        def tr(e):
            for q in range(4):
                e.matmul(ps_t[:, q * 8:(q + 1) * 8],
                         rlog[:, q * 128:(q + 1) * 128], ident8[:],
                         start=(q == 0), stop=(q == 3), is_transpose=True)
        t_tr = p.step("pe", tr, waits=[t_evr] + tw + p.wr("l0f_psax"))
        p.rd("l0f_rlog", t_tr)
        t_evt = p.step("act", lambda e: e.copy(out=ltok[:], in_=ps_t[:, 0:32]),
                       waits=[t_tr] + p.wr("l0f_ltok"))
        p.rd("l0f_psax", t_evt)

        def topk(e):
            for q in range(4):
                l = ltok[:, q, :]
                e.reduce_max(col[:, 0:1], l, axis=AX.X)
                e.drain()
                e.tensor_scalar(out=mtok[:, q, :], in0=l, scalar1=col[:, 0:1],
                                scalar2=None, op0=ALU.is_ge)
                e.drain()
                e.tensor_scalar_mul(col[:, 4 + q:5 + q], col[:, 0:1], -1.0)
                e.drain()
                e.scalar_tensor_tensor(out=mtok[:, q, :], in0=mtok[:, q, :],
                                       scalar=-1e30, in1=l,
                                       op0=ALU.mult, op1=ALU.add)
                e.drain()
                e.reduce_max(col[:, 1:2], mtok[:, q, :], axis=AX.X)
                e.drain()
                e.tensor_scalar(out=mtok[:, q, :], in0=l, scalar1=col[:, 1:2],
                                scalar2=None, op0=ALU.is_ge)
                e.drain()
        t_tk = p.step("dve", topk, waits=[t_evt] + p.wr("l0f_mtok"))

        def expl(e):
            for q in range(4):
                e.activation(out=ltok[:, q, :], in_=ltok[:, q, :], func=AF.Exp,
                             bias=col[:, 4 + q:5 + q])
        t_exp = p.step("act", expl, waits=[t_tk])
        p.rd("l0f_ltok", t_exp)

        def wfin(e):
            for q in range(4):
                e.tensor_mul(ltok[:, q, :], ltok[:, q, :], mtok[:, q, :])
                e.drain()
                e.reduce_sum(col[:, 2:3], ltok[:, q, :], axis=AX.X)
                e.drain()
                e.reciprocal(col[:, 2:3], col[:, 2:3])
                e.drain()
                e.tensor_scalar_mul(col[:, 2:3], col[:, 2:3], 2.0)
                e.drain()
                e.tensor_scalar(out=wtok[:, q, :], in0=ltok[:, q, :],
                                scalar1=col[:, 2:3], scalar2=None, op0=ALU.mult)
                e.drain()
        t_wf = p.step("dve", wfin, waits=[t_exp] + p.wr("l0f_wtok"))
        p.rd("l0f_mtok", t_wf)
        p.rd("l0f_ltok", t_wf)

        def trb(e):
            for q in range(4):
                e.matmul(ps_r[:, q * 128:(q + 1) * 128], wtok[:, q, :],
                         cm.ident[:], start=(q == 0), stop=(q == 3),
                         is_transpose=True)
        t_trb = p.step("pe", trb, waits=[t_wf] + cm.ready + p.wr("l0f_psr"))
        p.rd("l0f_wtok", t_trb)
        t_evw = p.step("act", lambda e: e.copy(out=wrow[:], in_=ps_r[:]),
                       waits=[t_trb] + p.wr("l0f_wrow"))
        p.rd("l0f_psr", t_evw)
        t_bcw = p.step("pe", lambda e: e.matmul(
            ps_w16[:], rep8[:], wrow[:], start=True, stop=True),
            waits=[t_evw] + tw + p.wr("l0f_psy1"))
        p.rd("l0f_wrow", t_bcw)

        def mm_ax(e, xl=xl):
            for c in range(NCH):
                e.matmul(ps_ax[:], A_all[:, c, :], xl[:, c, :],
                         start=(c == 0), stop=(c == NCH - 1))
        t_ax = p.step("pe", mm_ax, waits=[t_xln] + tw + p.wr("l0f_psax"))
        t_evax = p.step("act", lambda e: e.copy(out=ax[:], in_=ps_ax[:]),
                        waits=[t_ax] + p.wr("l0f_ax"))
        p.rd("l0f_psax", t_evax)
        t_wax = p.step("dve", lambda e: e.tensor_mul(wax[:], ax[:], ps_w16[:]),
                       waits=[t_evax, t_bcw] + p.wr("l0f_wax"))
        p.rd("l0f_ax", t_wax)
        p.rd("l0f_psy1", t_wax)

        yt, yw, yi = ytile.acquire()
        evs = []
        for oc in range(NCH):
            pso = ps_y[oc % 2]

            def mm_y(e, oc=oc, pso=pso, xl=xl):
                for c in range(NCH):
                    e.matmul(pso[:], base_w[:, c, oc * 128:(oc + 1) * 128],
                             xl[:, c, :], start=(c == 0), stop=False)
                e.matmul(pso[:], B_stk[:, oc * 128:(oc + 1) * 128], wax[:],
                         start=False, stop=True)
            t_mm = p.step("pe", mm_y,
                          waits=[t_xln, t_wax] + tw + p.wr(f"l0f_psy{oc % 2}"))

            def ev_y(e, oc=oc, pso=pso, yt=yt):
                e.activation(out=yt[:, oc, :], in_=pso[:], func=AF.Identity,
                             bias=base_b[:, oc:oc + 1])
            t_ev = p.step("act", ev_y, waits=[t_mm] + yw)
            p.rd(f"l0f_psy{oc % 2}", t_ev)
            evs.append(t_ev)
            yw = []
        p.rd("l0f_wax", evs[-1])
        xln.read(xli, t_r, t_ax, *evs)

        if dbg is not None and tt == 0:
            d1 = p.dma("pool", lambda e: e.dma_start(out=dbg["rlog"], in_=rlog[:]),
                       waits=[t_evr])
            p.rd("l0f_rlog", d1)
            d2 = p.dma("pool", lambda e: e.dma_start(out=dbg["wrow"], in_=wrow[:]),
                       waits=[t_evw])
            p.rd("l0f_wrow", d2)
            d3 = p.dma("pool", lambda e: e.dma_start(out=dbg["ax"], in_=ax[:]),
                       waits=[t_evax])
            p.rd("l0f_ax", d3)
            d4 = p.dma("pool", lambda e: e.dma_start(out=dbg["yt"], in_=yt[:]),
                       waits=evs)
            ytile.read(yi, d4)
            d5 = p.dma("pool", lambda e: e.dma_start(out=dbg["xl"], in_=xl[:]),
                       waits=[t_xln])
            xln.read(xli, d5)
        t_st, ycons = emit_resid_out_tile(p, cm, yt, evs[-1], x_in, x_out, tt)
        ytile.read(yi, *ycons)
        ytile.read(yi, *evs)
    p.barrier()



T2 = 2048
NST = T2 // TT
NT2 = S // T2


class MoeScratch:
    def __init__(self, p, nexp, blk, FU, glu=False):
        self.nexp = nexp
        self.blk = blk
        self.FU = FU
        self.nsub = FU // 128
        E = nexp
        self.rlog = p.sb(blk + "_rlog", [E, TT], F32)
        self.ltok = p.sb(blk + "_ltok", [128, 4, E], F32)
        self.mtok = p.sb(blk + "_mtok", [128, 4, E], F32)
        self.pm = p.sb(blk + "_pm", [128, 4, E], F32)
        self.col = p.sb(blk + "_col", [128, 8], F32)
        self.cwt = p.sb(blk + "_cwt", [128, NST, 4, E + 1], F32)
        self.xlt = p.sb(blk + "_xlt", [128, NCH, T2], BF16)
        self.yT = p.sb(blk + "_yT", [128, NST * 4, D], BF16)
        self.glu = glu
        if glu:
            self.wg = Rot([p.sb(blk + f"_wg{i}", [128, NCH, FU], BF16)
                           for i in range(2)])
            self.wu = Rot([p.sb(blk + f"_wu{i}", [128, NCH, FU], BF16)
                           for i in range(2)])
            self.hgt = p.sb(blk + "_hgt", [128, self.nsub, TT], F32)
        else:
            self.w1 = Rot([p.sb(blk + f"_w1{i}", [128, NCH, FU], BF16)
                           for i in range(2)])
            self.b1c = Rot([p.sb(blk + f"_b1c{i}", [128, self.nsub], F32)
                            for i in range(2)])
        self.w2 = Rot([p.sb(blk + f"_w2{i}", [128, self.nsub, D], BF16)
                       for i in range(2)])
        self.b2r = Rot([p.sb(blk + "_b2r0", [1, D], BF16)])
        self.hst = p.sb(blk + "_h", [128, self.nsub, TT], BF16)
        self.hp = [p.psum(blk + f"_hp{i}", [128, TT]) for i in range(2)]
        self.yp = [p.psum(blk + f"_yp{i}", [128, TT]) for i in range(2)]
        self.pbf = p.psum(blk + "_pbf", [128, 1024], BF16)
        self.identE = p.sb(blk + "_idE", [E, E], F32)
        self.t_idE = p.dma("sp", lambda e: e.dma_start(
            out=self.identE[:], in_=p.aps["ident" + str(E)]))


def emit_router_tile(p, cm, ms, variant, rw, rb_col, t_rw, xl_view, t_xl, st,
                     scale=1.0):
    E = ms.nexp
    B = ms.blk
    ps_log = ms.hp[0]
    ps_tr = ms.hp[1]

    def mm_l(e):
        for c in range(NCH):
            e.matmul(ps_log[0:E, :], rw[:, c, :], xl_view[:, c, :],
                     start=(c == 0), stop=(c == NCH - 1))
    t_l = p.step("pe", mm_l, waits=[t_xl] + list(t_rw) + p.wr(B + "_hp0"))

    if rb_col is not None:
        t_evl = p.step("act", lambda e: e.activation(
            out=ms.rlog[:], in_=ps_log[0:E, :], func=AF.Identity,
            bias=rb_col[:, 0:1]), waits=[t_l] + p.wr(B + "_rlog"))
    else:
        t_evl = p.step("act", lambda e: e.copy(
            out=ms.rlog[:], in_=ps_log[0:E, :]),
            waits=[t_l] + p.wr(B + "_rlog"))
    p.rd(B + "_hp0", t_evl)

    def tr(e):
        for q in range(4):
            e.matmul(ps_tr[0:128, q * E:(q + 1) * E],
                     ms.rlog[:, q * 128:(q + 1) * 128], ms.identE[:],
                     start=(q == 0), stop=(q == 3), is_transpose=True)
    t_tr = p.step("pe", tr, waits=[t_evl, ms.t_idE] + p.wr(B + "_hp1"))
    p.rd(B + "_rlog", t_tr)
    t_evt = p.step("act", lambda e: e.copy(
        out=ms.ltok[:], in_=ps_tr[:, 0:4 * E]),
        waits=[t_tr] + p.wr(B + "_ltok"))
    p.rd(B + "_hp1", t_evt)

    def maxes(e):
        for q in range(4):
            e.reduce_max(ms.col[:, 0:1], ms.ltok[:, q, :], axis=AX.X)
            e.drain()
            e.tensor_scalar_mul(ms.col[:, 4 + q:5 + q], ms.col[:, 0:1], -1.0)
            e.drain()
    t_mx = p.step("dve", maxes, waits=[t_evt])

    def expl(e):
        for q in range(4):
            e.activation(out=ms.ltok[:, q, :], in_=ms.ltok[:, q, :],
                         func=AF.Exp, bias=ms.col[:, 4 + q:5 + q])
    t_exp = p.step("act", expl, waits=[t_mx])
    p.rd(B + "_ltok", t_exp)

    def wsel(e):
        for q in range(4):
            et = ms.ltok[:, q, :]
            mt = ms.mtok[:, q, :]
            pq = ms.pm[:, q, :]
            if variant == "top2_plain":
                e.reduce_sum(ms.col[:, 2:3], et, axis=AX.X)
                e.drain()
            e.tensor_scalar(out=mt, in0=et, scalar1=1.0, scalar2=None,
                            op0=ALU.is_ge)
            e.drain()
            e.scalar_tensor_tensor(out=mt, in0=mt, scalar=-1e30, in1=et,
                                   op0=ALU.mult, op1=ALU.add)
            e.drain()
            e.reduce_max(ms.col[:, 1:2], mt, axis=AX.X)
            e.drain()
            if variant == "top3_sum":
                e.tensor_scalar(out=pq, in0=mt, scalar1=ms.col[:, 1:2],
                                scalar2=None, op0=ALU.is_ge)
                e.drain()
                e.scalar_tensor_tensor(out=pq, in0=pq, scalar=-1e30, in1=mt,
                                       op0=ALU.mult, op1=ALU.add)
                e.drain()
                e.reduce_max(ms.col[:, 1:2], pq, axis=AX.X)
                e.drain()
            e.tensor_scalar(out=mt, in0=et, scalar1=ms.col[:, 1:2],
                            scalar2=None, op0=ALU.is_ge)
            e.drain()
            e.tensor_mul(et, et, mt)
            e.drain()
            if variant != "top2_plain":
                e.reduce_sum(ms.col[:, 2:3], et, axis=AX.X)
                e.drain()
            e.reciprocal(ms.col[:, 2:3], ms.col[:, 2:3])
            e.drain()
            if scale != 1.0:
                e.tensor_scalar_mul(ms.col[:, 2:3], ms.col[:, 2:3], scale)
                e.drain()
            e.tensor_scalar(out=ms.cwt[:, st, q, 0:E], in0=et,
                            scalar1=ms.col[:, 2:3], scalar2=None, op0=ALU.mult)
            e.drain()
    t_w = p.step("dve", wsel, waits=[t_exp] + p.wr(B + f"_cwt{st}"))
    p.rd(B + "_ltok", t_w)
    p.rd(B + "_mtok", t_w)
    p.rd(B + "_pm", t_w)
    return t_w


def emit_moe_unit(p, cm, ms, cfg, t_cw, t_xlt, yw):
    B = ms.blk
    name = cfg["name"]
    nsub = ms.nsub
    glu = ms.glu

    if glu:
        wg, wgw, wgi = ms.wg.acquire()
        wu, wuw, wui = ms.wu.acquire()
        t_wg = p.dma("pool", lambda e: e.dma_start(
            out=wg[:], in_=cfg["wg_ap"].rearrange("(c p) f -> p c f", p=128)),
            waits=wgw)
        t_wu = p.dma("pool", lambda e: e.dma_start(
            out=wu[:], in_=cfg["wu_ap"].rearrange("(c p) f -> p c f", p=128)),
            waits=wuw)
        t_w1 = [t_wg, t_wu]
    else:
        w1, w1w, w1i = ms.w1.acquire()
        t_w1 = [p.dma("pool", lambda e: e.dma_start(
            out=w1[:], in_=cfg["w1_ap"].rearrange("(c p) f -> p c f", p=128)),
            waits=w1w)]
    w2, w2w, w2i = ms.w2.acquire()
    t_w2 = p.dma("pool", lambda e: e.dma_start(
        out=w2[:], in_=cfg["w2_ap"].rearrange("(s p) o -> p s o", p=128)),
        waits=w2w)
    tb = []
    b1c = b2r = None
    if cfg.get("b1col_ap") is not None:
        b1c, b1w, b1i = ms.b1c.acquire()
        tb.append(p.dma("sp", lambda e: e.dma_start(
            out=b1c[:], in_=cfg["b1col_ap"]), waits=b1w))
    if cfg.get("b2row_ap") is not None:
        b2r, b2w, b2i = ms.b2r.acquire()
        tb.append(p.dma("pool", lambda e: e.dma_start(
            out=b2r[:], in_=cfg["b2row_ap"]), waits=b2w))

    hst = ms.hst
    evs = []
    hreaders = []
    for st in range(NST):
        t_h = []
        for s_ in range(nsub):
            hpb = ms.hp[s_ % 2]
            if glu:
                def mm_g(e, s_=s_, hpb=hpb, st=st, wg=wg):
                    for c in range(NCH):
                        e.matmul(hpb[:], wg[:, c, s_ * 128:(s_ + 1) * 128],
                                 ms.xlt[:, c, st * TT:(st + 1) * TT],
                                 start=(c == 0), stop=(c == NCH - 1))
                t_mm = p.step("pe", mm_g,
                              waits=[t_xlt] + t_w1 + p.wr(B + f"_hp{s_ % 2}"))

                def ev_g(e, s_=s_, hpb=hpb):
                    e.activation(out=ms.hgt[:, s_, :], in_=hpb[:], func=AF.Silu)
                t_evg = p.step("act", ev_g,
                               waits=[t_mm] + p.wr(B + f"_hg{s_}"))
                p.rd(B + f"_hp{s_ % 2}", t_evg)

                def mm_u(e, s_=s_, hpb=hpb, st=st, wu=wu):
                    for c in range(NCH):
                        e.matmul(hpb[:], wu[:, c, s_ * 128:(s_ + 1) * 128],
                                 ms.xlt[:, c, st * TT:(st + 1) * TT],
                                 start=(c == 0), stop=(c == NCH - 1))
                t_mm2 = p.step("pe", mm_u, waits=[t_xlt, t_evg] + t_w1)

                def ev_m(e, s_=s_, hpb=hpb):
                    e.tensor_mul(hst[:, s_, :], ms.hgt[:, s_, :], hpb[:])
                t_ev = p.step("dve", ev_m,
                              waits=[t_mm2] + p.wr(B + f"_h{s_}"))
                p.rd(B + f"_hp{s_ % 2}", t_ev)
                p.rd(B + f"_hg{s_}", t_ev)
            else:
                def mm_h(e, s_=s_, hpb=hpb, st=st, w1=w1):
                    for c in range(NCH):
                        e.matmul(hpb[:], w1[:, c, s_ * 128:(s_ + 1) * 128],
                                 ms.xlt[:, c, st * TT:(st + 1) * TT],
                                 start=(c == 0), stop=(c == NCH - 1))
                t_mm = p.step("pe", mm_h,
                              waits=[t_xlt] + t_w1 + p.wr(B + f"_hp{s_ % 2}"))
                af = AF.Silu if cfg["act"] == "silu" else AF.Relu
                if b1c is not None:
                    def ev_h(e, s_=s_, hpb=hpb, af=af, b1c=b1c):
                        e.activation(out=hst[:, s_, :], in_=hpb[:], func=af,
                                     bias=b1c[:, s_:s_ + 1])
                else:
                    def ev_h(e, s_=s_, hpb=hpb, af=af):
                        e.activation(out=hst[:, s_, :], in_=hpb[:], func=af)
                t_ev = p.step("act", ev_h,
                              waits=[t_mm] + tb + p.wr(B + f"_h{s_}"))
                p.rd(B + f"_hp{s_ % 2}", t_ev)
            t_h.append(t_ev)

        for q in range(4):
            for och in range(2):
                ypb = ms.yp[och]

                def mm_y(e, q=q, och=och, ypb=ypb, w2=w2, b2r=b2r):
                    for s_ in range(nsub):
                        last = (s_ == nsub - 1) and b2r is None
                        e.matmul(ypb[:], hst[:, s_, q * 128:(q + 1) * 128],
                                 w2[:, s_, och * TT:(och + 1) * TT],
                                 start=(s_ == 0), stop=last)
                    if b2r is not None:
                        e.matmul(ypb[:], cm.ones1bf[:],
                                 b2r[:, och * TT:(och + 1) * TT],
                                 start=False, stop=True)
                t_mm = p.step("pe", mm_y,
                              waits=t_h + [t_w2] + tb + p.wr(B + f"_yp{och}"))
                for s_ in range(nsub):
                    p.rd(B + f"_h{s_}", t_mm)

                cw_ap = ms.cwt[:, st, q, cfg["cw_idx"]:cfg["cw_idx"] + 1]

                def ev_y(e, q=q, och=och, ypb=ypb, st=st, cw_ap=cw_ap):
                    e.scalar_tensor_tensor(
                        out=ms.yT[:, st * 4 + q, och * TT:(och + 1) * TT],
                        in0=ypb[:], scalar=cw_ap,
                        in1=ms.yT[:, st * 4 + q, och * TT:(och + 1) * TT],
                        op0=ALU.mult, op1=ALU.add)
                t_ev = p.step("dve", ev_y,
                              waits=[t_mm] + list(t_cw) + list(yw))
                yw = []
                p.rd(B + f"_yp{och}", t_ev)
                p.rd(B + f"_cwt{st}", t_ev)
                evs.append(t_ev)
    if glu:
        ms.wg.read(wgi, *evs)
        ms.wu.read(wui, *evs)
    else:
        ms.w1.read(w1i, *evs)
        if b1c is not None:
            ms.b1c.read(b1i, *evs)
    ms.w2.read(w2i, *evs)
    if b2r is not None:
        ms.b2r.read(b2i, *evs)
    return evs


def emit_moe_resid(p, cm, ms, x_in, x_out, tglob, y_evs, extra_add=None,
                   extra_tok=()):
    B = ms.blk
    st = tglob % NST
    yf = ms.xlt[:].bitcast(F32)[:, :, 0:TT]
    trs = []
    for q in range(4):
        def tr(e, q=q, st=st):
            for oc in range(NCH):
                e.matmul(ms.pbf[:, oc * 128:(oc + 1) * 128],
                         ms.yT[:, st * 4 + q, oc * 128:(oc + 1) * 128],
                         cm.identbf[:], start=(oc == 0), stop=(oc == NCH - 1),
                         is_transpose=True)
        t_tr = p.step("pe", tr,
                      waits=list(y_evs) + cm.ready + p.wr(B + "_pbf"))

        if extra_add is not None:
            def ev(e, q=q):
                for oc in range(NCH):
                    e.tensor_add(yf[:, oc, q * 128:(q + 1) * 128],
                                 ms.pbf[:, oc * 128:(oc + 1) * 128],
                                 extra_add[:, oc, q * 128:(q + 1) * 128])
            t_ev = p.step("dve", ev,
                          waits=[t_tr] + list(extra_tok) + p.wr(B + "_yfq"))
        else:
            def ev(e, q=q):
                e.copy(out=yf[:, :, q * 128:(q + 1) * 128],
                       in_=ms.pbf[:].rearrange("p (c b) -> p c b", c=NCH))
            t_ev = p.step("act", ev, waits=[t_tr] + p.wr(B + "_yfq"))
        p.rd(B + "_pbf", t_ev)
        trs.append(t_ev)
        p.rd(B + "_yfq", t_ev)
    t_st, ycons = emit_resid_out_tile(p, cm, yf, trs[-1], x_in, x_out, tglob)
    for t in ycons:
        p.rd(B + "_yfq", t)
        p.rd(B + "_xlt", t)
    for t in trs:
        p.rd(B + "_xlt", t)
    return t_st, trs



def emit_L2_ffn(p, cm, aps, x_in, x_out):
    E = 16
    ms = MoeScratch(p, E, "l2f", FU=1024)
    rw = p.sb("l2f_rw", [128, NCH, E], BF16)
    t_rw = p.dma("pool", lambda e: e.dma_start(
        out=rw[:], in_=aps["L2f_gate"].rearrange("(c p) e -> p c e", p=128)))
    yT_w = []
    for t2 in range(NT2):
        t_cw = []
        t_xl = None
        for st in range(NST):
            tglob = t2 * NST + st
            t_xl, _ = emit_ln_in_tile(
                p, cm, x_in, tglob, ms.xlt[:, :, st * TT:(st + 1) * TT],
                p.wr("l2f_xlt") if st == 0 else [])
            p.rd("l2f_xlt", t_xl)
            t_cw.append(emit_router_tile(
                p, cm, ms, "top3_sum", rw, None, [t_rw],
                ms.xlt[:, :, st * TT:(st + 1) * TT], t_xl, st))
        t_z = p.step("pool", lambda e: e.memset(ms.yT[:], 0.0),
                     waits=yT_w + p.wr("l2f_yT"))
        yw = [t_z]
        evs = []
        for e_ in range(E):
            for half in range(2):
                cfg = {
                    "name": f"l2f_u{e_}_{half}", "cw_idx": e_, "act": "relu",
                    "w1_ap": aps["L2f_W1"][e_, :, half * 1024:(half + 1) * 1024],
                    "w2_ap": aps["L2f_W2"][e_, half * 1024:(half + 1) * 1024, :],
                    "b1col_ap": aps["L2f_b1c"][e_, :, half * 8:(half + 1) * 8],
                    "b2row_ap": aps["L2f_b2"][e_:e_ + 1, :] if half == 0 else None,
                }
                evs = emit_moe_unit(p, cm, ms, cfg, t_cw, t_xl, yw)
                yw = []
        yT_w = []
        for st in range(NST):
            tglob = t2 * NST + st
            t_st, trs = emit_moe_resid(p, cm, ms, x_in, x_out, tglob, evs)
            yT_w.extend(trs)
    p.barrier()


def emit_L3_ffn(p, cm, aps, x_in, x_out):
    E = 16
    ms = MoeScratch(p, E, "l3f", FU=512, glu=True)
    rw = p.sb("l3f_rw", [128, NCH, E], BF16)
    t_rw = p.dma("pool", lambda e: e.dma_start(
        out=rw[:], in_=aps["L3f_rwT"].rearrange("(c p) e -> p c e", p=128)))
    sgw = p.sb("l3f_sgw", [128, NCH], BF16)
    t_sgw = p.dma("pool", lambda e: e.dma_start(out=sgw[:], in_=aps["L3f_sgw"]))
    yT_w = []
    for t2 in range(NT2):
        t_cw = []
        t_xl = None
        for st in range(NST):
            tglob = t2 * NST + st
            t_xl, _ = emit_ln_in_tile(
                p, cm, x_in, tglob, ms.xlt[:, :, st * TT:(st + 1) * TT],
                p.wr("l3f_xlt") if st == 0 else [])
            p.rd("l3f_xlt", t_xl)
            t_r = emit_router_tile(
                p, cm, ms, "top2_sum", rw, None, [t_rw],
                ms.xlt[:, :, st * TT:(st + 1) * TT], t_xl, st)
            ps = ms.hp[0]

            def mm_sg(e, st=st):
                for c in range(NCH):
                    e.matmul(ps[0:1, :], sgw[:, c:c + 1],
                             ms.xlt[:, c, st * TT:(st + 1) * TT],
                             start=(c == 0), stop=(c == NCH - 1))
            t_sg = p.step("pe", mm_sg,
                          waits=[t_xl, t_sgw] + p.wr("l3f_hp0"))
            t_evs = p.step("act", lambda e, st=st: e.activation(
                out=ms.rlog[0:1, :], in_=ps[0:1, :], func=AF.Sigmoid),
                waits=[t_sg] + p.wr("l3f_rlog"))
            p.rd("l3f_hp0", t_evs)

            def trsg(e, st=st):
                for q in range(4):
                    e.matmul(ms.hp[1][0:128, q:q + 1],
                             ms.rlog[0:1, q * 128:(q + 1) * 128],
                             ms.identE[0:1, 0:1],
                             start=(q == 0), stop=(q == 3), is_transpose=True)
            t_trs = p.step("pe", trsg,
                           waits=[t_evs, ms.t_idE] + p.wr("l3f_hp1"))
            p.rd("l3f_rlog", t_trs)

            def evsg(e, st=st):
                for q in range(4):
                    e.copy(out=ms.cwt[:, st, q, E:E + 1], in_=ms.hp[1][:, q:q + 1])
            t_evsg = p.step("act", evsg, waits=[t_trs, t_r])
            p.rd("l3f_hp1", t_evsg)
            p.rd(f"l3f_cwt{st}", t_evsg)
            t_cw.extend([t_r, t_evsg])
        t_z = p.step("pool", lambda e: e.memset(ms.yT[:], 0.0),
                     waits=yT_w + p.wr("l3f_yT"))
        yw = [t_z]
        evs = []
        for e_ in range(E):
            cfg = {
                "name": f"l3f_u{e_}", "cw_idx": e_,
                "wg_ap": aps["L3f_Wg"][e_], "wu_ap": aps["L3f_Wu"][e_],
                "w2_ap": aps["L3f_Wd"][e_],
            }
            evs = emit_moe_unit(p, cm, ms, cfg, t_cw, t_xl, yw)
            yw = []
        for half in range(4):
            cfg = {
                "name": f"l3f_sh{half}", "cw_idx": E,
                "wg_ap": aps["L3f_Sg"][:, half * 512:(half + 1) * 512],
                "wu_ap": aps["L3f_Su"][:, half * 512:(half + 1) * 512],
                "w2_ap": aps["L3f_Sd"][half * 512:(half + 1) * 512, :],
            }
            evs = emit_moe_unit(p, cm, ms, cfg, t_cw, t_xl, yw)
        yT_w = []
        for st in range(NST):
            tglob = t2 * NST + st
            t_st, trs = emit_moe_resid(p, cm, ms, x_in, x_out, tglob, evs)
            yT_w.extend(trs)
    p.barrier()


def emit_L1_ffn(p, cm, aps, x_in, x_out, xmid_dram):
    E = 8
    ms = MoeScratch(p, E, "l1f", FU=1024)
    rw = p.sb("l1f_rw", [128, NCH, E], BF16)
    t_rw = p.dma("pool", lambda e: e.dma_start(
        out=rw[:], in_=aps["L1f_rw"].rearrange("(c p) e -> p c e", p=128)))
    mixw = ms.w1.tiles[0]
    t_mx = p.dma("pool", lambda e: e.dma_start(
        out=mixw[:], in_=aps["L1f_mixw"].rearrange("(c p) o -> p c o", p=128)))
    g1c = p.sb("l1f_g1c", [128, NCH], F32)
    be1c = p.sb("l1f_be1c", [128, NCH], F32)
    g2c = p.sb("l1f_g2c", [128, NCH], F32)
    be2c = p.sb("l1f_be2c", [128, NCH], F32)
    mixbc = p.sb("l1f_mixbc", [128, NCH], F32)
    tgb = [
        p.dma("sp", lambda e: e.dma_start(out=g1c[:], in_=aps["L1f_g1c"])),
        p.dma("sp", lambda e: e.dma_start(out=be1c[:], in_=aps["L1f_be1c"])),
        p.dma("sp", lambda e: e.dma_start(out=g2c[:], in_=aps["L1f_g2c"])),
        p.dma("sp", lambda e: e.dma_start(out=be2c[:], in_=aps["L1f_be2c"])),
        p.dma("sp", lambda e: e.dma_start(out=mixbc[:], in_=aps["L1f_mixbc"])),
    ]
    xltf = ms.xlt[:].bitcast(F32)
    yTf = ms.yT[:].bitcast(F32)
    uf = xltf[:, :, 0:TT]
    n1f = xltf[:, :, TT:2 * TT]
    n1b = ms.yT[:, 0:NCH, 0:TT]
    xmid = yTf[:, NCH:2 * NCH, :]

    for tt in range(NTT):
        t_u, _ = emit_ln_in_tile(p, cm, x_in, tt, None, p.wr("l1f_uf"),
                                 out_f32=uf)
        p.rd("l1f_uf", t_u)
        t_n1, t_n1b = emit_ln_core(p, cm, uf, [t_u], out_bf=n1b, out_f32=n1f,
                                   out_waits=p.wr("l1f_n1"), gain_col=g1c,
                                   bias_col=be1c, gb_waits=tgb)
        p.rd("l1f_n1", t_n1b)
        for oc in range(NCH):
            ypb = ms.yp[oc % 2]

            def mm(e, oc=oc, ypb=ypb):
                for c in range(NCH):
                    e.matmul(ypb[:], mixw[:, c, oc * 128:(oc + 1) * 128],
                             n1b[:, c, :], start=(c == 0), stop=(c == NCH - 1))
            t_mm = p.step("pe", mm,
                          waits=[t_n1b, t_mx] + p.wr(f"l1f_yp{oc % 2}"))
            p.rd("l1f_n1", t_mm)

            def ev(e, oc=oc, ypb=ypb):
                e.activation(out=xmid[:, oc, :], in_=ypb[:], func=AF.Identity,
                             bias=mixbc[:, oc:oc + 1])
            t_ev = p.step("act", ev,
                          waits=[t_mm] + tgb + p.wr(f"l1f_xm{oc}"))
            p.rd(f"l1f_yp{oc % 2}", t_ev)

            def add1(e, oc=oc):
                e.tensor_add(xmid[:, oc, :], xmid[:, oc, :], uf[:, oc, :])
            t_a1 = p.step("dve", add1, waits=[t_ev, t_u])
            p.rd("l1f_uf", t_a1)

            def add2(e, oc=oc):
                e.tensor_add(xmid[:, oc, :], xmid[:, oc, :], n1f[:, oc, :])
            t_a2 = p.step("pool", add2, waits=[t_a1, t_n1])
            p.rd("l1f_n1", t_a2)
            p.rd(f"l1f_xm{oc}", t_a2)
        t_stx = p.dma("act", lambda e, tt=tt: e.dma_start(
            out=x_ap(xmid_dram, tt * TT, TT), in_=xmid), waits=[t_a2])
        for oc in range(NCH):
            p.rd(f"l1f_xm{oc}", t_stx)
    p.barrier()

    yT_w = []
    for t2 in range(NT2):
        t_cw = []
        t_xl = None
        for st in range(NST):
            tglob = t2 * NST + st
            t_xl, _ = emit_ln_in_tile(
                p, cm, xmid_dram, tglob, ms.xlt[:, :, st * TT:(st + 1) * TT],
                p.wr("l1f_xlt") if st == 0 else [], gain_col=g2c,
                bias_col=be2c, gb_waits=tgb)
            p.rd("l1f_xlt", t_xl)
            t_cw.append(emit_router_tile(
                p, cm, ms, "top2_plain", rw, None, [t_rw],
                ms.xlt[:, :, st * TT:(st + 1) * TT], t_xl, st))
        t_z = p.step("pool", lambda e: e.memset(ms.yT[:], 0.0),
                     waits=yT_w + p.wr("l1f_yT"))
        yw = [t_z]
        evs = []
        for e_ in range(E):
            for half in range(4):
                cfg = {
                    "name": f"l1f_u{e_}_{half}", "cw_idx": e_, "act": "silu",
                    "w1_ap": aps["L1f_w1"][e_, :, half * 1024:(half + 1) * 1024],
                    "w2_ap": aps["L1f_w2"][e_, half * 1024:(half + 1) * 1024, :],
                    "b1col_ap": aps["L1f_b1c"][e_, :, half * 8:(half + 1) * 8],
                    "b2row_ap": aps["L1f_b2"][e_:e_ + 1, :] if half == 0 else None,
                }
                evs = emit_moe_unit(p, cm, ms, cfg, t_cw, t_xl, yw)
                yw = []
        yT_w = []
        for st in range(NST):
            tglob = t2 * NST + st
            xmid2 = xltf[:, :, TT:2 * TT]
            t_xm = p.dma("sp", lambda e, tglob=tglob: e.dma_start(
                out=xmid2, in_=x_ap(xmid_dram, tglob * TT, TT)),
                waits=p.wr("l1f_xmid2") + list(evs))
            t_st, trs = emit_moe_resid(p, cm, ms, x_in, x_out, tglob, evs,
                                       extra_add=xmid2, extra_tok=[t_xm])
            p.rd("l1f_xmid2", trs[-1])
            p.rd("l1f_xlt", t_xm)
            yT_w.extend(trs)
    p.barrier()



def emit_proj_featmajor(p, cm, name, wt, t_wt, bcol, t_b, xl, t_xl, out_bf,
                        out_waits, nchunk_out, scale=None, square_to=None,
                        sq_res=None, sq_base=0):
    evs = []
    for oc in range(nchunk_out):
        pb = p.prj[oc % 2]

        def mm(e, oc=oc, pb=pb):
            for c in range(NCH):
                e.matmul(pb[:], wt[:, c, oc * 128:(oc + 1) * 128],
                         xl[:, c, :], start=(c == 0), stop=(c == NCH - 1))
        t_mm = p.step("pe", mm,
                      waits=[t_xl] + list(t_wt) + p.wr(f"prj{oc % 2}"))

        if scale is not None:
            def ev(e, oc=oc, pb=pb):
                e.activation(out=out_bf[:, oc, :], in_=pb[:], func=AF.Identity,
                             scale=scale, bias=bcol[:, oc:oc + 1])
        else:
            def ev(e, oc=oc, pb=pb):
                e.activation(out=out_bf[:, oc, :], in_=pb[:], func=AF.Identity,
                             bias=bcol[:, oc:oc + 1])
        t_ev = p.step("act", ev,
                      waits=[t_mm] + list(t_b) + list(out_waits))
        out_waits = []
        p.rd(f"prj{oc % 2}", t_ev)
        evs.append(t_ev)
        if square_to is not None:
            def sq(e, oc=oc):
                e.activation(out=square_to[:, oc, :], in_=out_bf[:, oc, :],
                             func=AF.Square)
            t_sq = p.step("act", sq,
                          waits=[t_ev] + (p.wr(f"{sq_res}{sq_base + oc}")
                                          if sq_res else []))
            evs.append(t_sq)
    return evs


def emit_L2_attn(p, cm, aps, x_in, x_out, dbg=None):
    QH, KVH, HD = 16, 4, 64
    attnout = p.nc.dram_tensor("l2a_attnout", [D, S], BF16, kind="Internal").ap()
    wq = p.sb("l2a_wq", [128, NCH, D], BF16)
    wk = p.sb("l2a_wk", [128, NCH, 256], BF16)
    wv = p.sb("l2a_wv", [128, NCH, 256], BF16)
    wo = wq
    bqc = p.sb("l2a_bqc", [128, NCH], F32)
    bkc = p.sb("l2a_bkc", [128, 2], F32)
    boc = p.sb("l2a_boc", [128, NCH], F32)
    bvr = p.sb("l2a_bvr", [1, 256], BF16)
    hsplit = p.sb("l2a_hsplit", [128, 2], BF16)
    tw = [
        p.dma("pool", lambda e: e.dma_start(
            out=wq[:], in_=aps["L2a_wqT"].rearrange("(c p) o -> p c o", p=128))),
        p.dma("pool", lambda e: e.dma_start(
            out=wk[:], in_=aps["L2a_wkT"].rearrange("(c p) o -> p c o", p=128))),
        p.dma("pool", lambda e: e.dma_start(
            out=wv[:], in_=aps["L2a_wvT"].rearrange("(c p) o -> p c o", p=128))),
        p.dma("sp", lambda e: e.dma_start(out=bqc[:], in_=aps["L2a_bqc"])),
        p.dma("sp", lambda e: e.dma_start(out=bkc[:], in_=aps["L2a_bkc"])),
        p.dma("sp", lambda e: e.dma_start(out=boc[:], in_=aps["L2a_boc"])),
        p.dma("pool", lambda e: e.dma_start(out=bvr[:], in_=aps["L2a_bv"][None, :])),
        p.dma("pool", lambda e: e.dma_start(out=hsplit[:], in_=aps["hsplit"])),
    ]
    t_bq = p.step("dve", lambda e: e.tensor_scalar_mul(bqc[:], bqc[:], 0.125),
                  waits=tw)

    Qb = p.sb("l2a_Q", [128, NCH, S], BF16)
    Kz = p.sb("l2a_K", [128, 4, S], BF16)
    Vt = p.sb("l2a_V", [128, S // 128, KVH * 65], BF16)
    qsq = p.sb("l2a_qsq", [128, NCH, TT], BF16)
    ksq = qsq[:, 6:8, :]
    nrm = p.sb("l2a_nrm", [1, 2 * QH + 2], F32)
    runq = p.sb("l2a_runq", [2, NCH + 1], F32)
    runk = p.sb("l2a_runk", [2, 5], F32)
    negMc = p.sb("l2a_negM", [128, QH], F32)
    p.prj = [p.psum("l2a_prj0", [128, TT]), p.psum("l2a_prj1", [128, TT])]
    ps_av = p.psum("l2a_av", [65, TT])
    ps_n = ps_av[0:2, :]
    ps_v = p.psum("l2a_psv", [128, 256])

    t_ones = p.step("pool", lambda e: e.memset(
        Vt[:].rearrange("p t (h x) -> p t h x", h=KVH)[:, :, :, 64:65], 1.0))
    t_kz = p.step("pool", lambda e: e.engine_nop())
    t_nz = p.step("dve", lambda e: (e.memset(nrm[:], 0.0),
                                     e.memset(runq[:], 0.0),
                                     e.memset(runk[:], 0.0)))

    nsq = p.sb("l2a_nsq", [2, TT], F32)
    _xflat = qsq[:].rearrange("p c t -> p (c t)")
    _xf32 = _xflat.bitcast(F32)
    Pb = [_xflat[:, 0:TT], _xflat[:, TT:2 * TT]]
    avt = _xf32[0:64, TT:2 * TT]
    rsum = _xf32[0:1, 2 * TT:3 * TT]
    obf = _xflat[0:64, 6 * TT:7 * TT]
    _qv = Qb[:].bitcast(F32)
    ytile = _qv[:, :, 0:TT]
    aot = Rot([Qb[:, :, 2 * TT * 2 + i * TT:2 * TT * 2 + (i + 1) * TT]
               for i in range(2)])

    qevs, kevs, vevs = [], [], []
    for tt in range(NTT):
        xl = cm.xbf
        t_xl, _ = emit_ln_in_tile(p, cm, x_in, tt, xl, [])

        evq = emit_proj_featmajor(p, cm, "q", wq, tw + [t_bq], bqc, [t_bq],
                                  xl, t_xl, Qb[:, :, tt * TT:(tt + 1) * TT],
                                  p.wr("l2a_Qs"), NCH, scale=0.125,
                                  square_to=qsq, sq_res="l2a_qsq")
        for oc in range(NCH):
            t_sq = evq[2 * oc + 1]

            def mmn(e, oc=oc):
                e.matmul(ps_n, hsplit[:], qsq[:, oc, :], start=True,
                         stop=True)
            t_n = p.step("pe", mmn, waits=[t_sq] + p.wr("l2a_psn"))
            p.rd(f"l2a_qsq{oc}", t_n)

            def evn(e):
                e.copy(out=nsq[:], in_=ps_n)
            t_evn = p.step("act", evn, waits=[t_n] + p.wr("l2a_nsq"))
            p.rd("l2a_psn", t_evn)

            def mxn(e, oc=oc):
                e.reduce_max(runq[:, NCH:NCH + 1], nsq[:], axis=AX.X)
                e.drain()
                e.tensor_tensor(runq[:, oc:oc + 1], runq[:, oc:oc + 1],
                                runq[:, NCH:NCH + 1], op=ALU.max)
                e.drain()
            t_mx = p.step("dve", mxn, waits=[t_evn, t_nz])
            p.rd("l2a_nsq", t_mx)
            qevs.append(t_mx)
        qevs.extend(evq)

        for kc in range(2):
            pb = p.prj[kc % 2]

            def mmk(e, kc=kc, pb=pb, xl=xl):
                for c in range(NCH):
                    e.matmul(pb[:], wk[:, c, kc * 128:(kc + 1) * 128],
                             xl[:, c, :], start=(c == 0), stop=(c == NCH - 1))
            t_mm = p.step("pe", mmk,
                          waits=[t_xl] + tw + p.wr(f"prj{kc % 2}"))

            def evk1(e, kc=kc, pb=pb, tt=tt):
                sl = slice(tt * TT, (tt + 1) * TT)
                e.activation(out=Kz[0:64, 2 * kc, sl], in_=pb[0:64, :],
                             func=AF.Identity, bias=bkc[0:64, kc:kc + 1])
                e.activation(out=Kz[64:128, 2 * kc, sl], in_=pb[0:64, :],
                             func=AF.Identity, bias=bkc[0:64, kc:kc + 1])
                e.activation(out=Kz[0:64, 2 * kc + 1, sl], in_=pb[64:128, :],
                             func=AF.Identity, bias=bkc[64:128, kc:kc + 1])
                e.activation(out=Kz[64:128, 2 * kc + 1, sl], in_=pb[64:128, :],
                             func=AF.Identity, bias=bkc[64:128, kc:kc + 1])
            t_ev = p.step("act", evk1, waits=[t_mm, t_kz] + tw)
            p.rd(f"prj{kc % 2}", t_ev)
            kevs.append(t_ev)
            for j in (2 * kc, 2 * kc + 1):
                def sqk(e, j=j, tt=tt):
                    e.activation(out=ksq[:, j % 2, :],
                                 in_=Kz[:, j, tt * TT:(tt + 1) * TT],
                                 func=AF.Square)
                t_sq = p.step("act", sqk,
                              waits=[t_ev] + p.wr(f"l2a_ksq{j % 2}"))

                def mmn2(e, j=j):
                    e.matmul(ps_n, hsplit[:], ksq[:, j % 2, :], start=True,
                             stop=True)
                t_n = p.step("pe", mmn2, waits=[t_sq] + p.wr("l2a_psn"))
                p.rd(f"l2a_ksq{j % 2}", t_n)
                t_evn = p.step("act", lambda e: e.copy(out=nsq[:], in_=ps_n),
                               waits=[t_n] + p.wr("l2a_nsq"))
                p.rd("l2a_psn", t_evn)

                def mxn2(e, j=j):
                    e.reduce_max(runk[:, 4:5], nsq[:], axis=AX.X)
                    e.drain()
                    e.tensor_tensor(runk[:, j:j + 1], runk[:, j:j + 1],
                                    runk[:, 4:5], op=ALU.max)
                    e.drain()
                t_mx = p.step("dve", mxn2, waits=[t_evn, t_nz])
                p.rd("l2a_nsq", t_mx)
                kevs.append(t_mx)

        for s_ in range(4):
            def mmv(e, s_=s_, tt=tt, xl=xl):
                for c in range(NCH):
                    e.matmul(ps_v[:], xl[:, c, s_ * 128:(s_ + 1) * 128],
                             wv[:, c, :], start=(c == 0), stop=False)
                e.matmul(ps_v[:], cm.ones1bf[:], bvr[:], start=False, stop=True)
            t_mv = p.step("pe", mmv,
                          waits=[t_xl] + tw + p.wr("l2a_psv"))

            def evv(e, s_=s_, tt=tt):
                e.copy(out=Vt[:, tt * 4 + s_, :].rearrange(
                    "p (h x) -> p h x", h=KVH)[:, :, 0:64],
                    in_=ps_v[:].rearrange("p (h x) -> p h x", h=KVH))
            t_ev = p.step("act", evv, waits=[t_mv] + p.wr("l2a_Vs"))
            p.rd("l2a_psv", t_ev)
            vevs.append(t_ev)
        for t in evq + kevs[-10:] + vevs[-4:]:
            p.rd("ln_src", t)

    t_drn = p.step("dve", lambda e: e.drain(), waits=qevs + kevs)
    t_mv = [
        p.dma("sp", lambda e: e.dma_start(
            out=nrm[0:1, 0:QH].rearrange("a (o h) -> a o h", h=2)[:, :, 0:1],
            in_=runq[0:1, 0:NCH], allow_slow_non_contiguous=True),
            waits=[t_drn]),
        p.dma("sp", lambda e: e.dma_start(
            out=nrm[0:1, 0:QH].rearrange("a (o h) -> a o h", h=2)[:, :, 1:2],
            in_=runq[1:2, 0:NCH], allow_slow_non_contiguous=True),
            waits=[t_drn]),
        p.dma("sp", lambda e: e.dma_start(
            out=nrm[0:1, QH:QH + 4], in_=runk[0:1, 0:4]), waits=[t_drn]),
    ]
    def mkM(e):
        for h in range(QH):
            e.tensor_mul(nrm[:, 2 * QH:2 * QH + 1], nrm[:, h:h + 1],
                         nrm[:, QH + h // 4:QH + h // 4 + 1])
            e.drain()
            e.tensor_copy(nrm[:, h:h + 1], nrm[:, 2 * QH:2 * QH + 1])
            e.drain()
    t_M1 = p.step("dve", mkM, waits=t_mv)
    t_M2 = p.step("act", lambda e: e.activation(
        out=nrm[:, 0:QH], in_=nrm[:, 0:QH], func=AF.Sqrt), waits=[t_M1])
    t_M3 = p.step("dve", lambda e: e.tensor_scalar_mul(
        nrm[:, 0:QH], nrm[:, 0:QH], -1.0), waits=[t_M2])
    t_M4 = p.step("pe", lambda e: e.matmul(
        p.prj[0][:, 0:QH], cm.ones1[:], nrm[:, 0:QH], start=True, stop=True),
        waits=[t_M3] + p.wr("prj0") + cm.ready)
    t_M = p.step("dve", lambda e: e.tensor_copy(negMc[:], p.prj[0][:, 0:QH]),
                 waits=[t_M4])
    p.rd("prj0", t_M)

    barr = p.barrier()
    if dbg is not None:
        p.dma("pool", lambda e: e.dma_start(out=dbg["Q"], in_=Qb[:]), waits=barr)
        p.dma("pool", lambda e: e.dma_start(out=dbg["Kd"], in_=Kb[:]), waits=barr)
        p.dma("pool", lambda e: e.dma_start(out=dbg["V"], in_=Vt[:]), waits=barr)
        p.dma("pool", lambda e: e.dma_start(out=dbg["negM"], in_=negMc[:]), waits=barr)
        barr = p.barrier()
    ps_sc = p.prj
    NKT = S // 128
    for h in range(QH):
        kvc = (h // 4) // 2
        kvr = ((h // 4) % 2) * 64
        qc = h // 2
        qr = (h % 2) * 64
        for qs in range(NTT):
            t_exps = []
            for kt in range(NKT):
                pb = ps_sc[kt % 2]
                pt = Pb[kt % 2]

                def mmsc(e, pb=pb, kt=kt, h=h, qr=qr, qc=qc, qs=qs):
                    e.matmul(pb[:],
                             Kz[qr:qr + 64, h // 4, kt * 128:(kt + 1) * 128],
                             Qb[qr:qr + 64, qc, qs * TT:(qs + 1) * TT],
                             start=True, stop=True)
                t_sc = p.step("pe", mmsc,
                              waits=[t_M] + barr + p.wr(f"prj{kt % 2}"))

                def expev(e, pb=pb, pt=pt, h=h):
                    e.activation(out=pt[:], in_=pb[:], func=AF.Exp,
                                 bias=negMc[:, h:h + 1])
                t_ex = p.step("act", expev,
                              waits=[t_sc, t_M] + p.wr(f"l2a_P{kt % 2}"))
                p.rd(f"prj{kt % 2}", t_ex)

                def mmav(e, pt=pt, kt=kt, h=h):
                    e.matmul(ps_av[:],
                             Vt[:, kt, (h // 4) * 65:(h // 4) * 65 + 65],
                             pt[:], start=(kt == 0), stop=(kt == NKT - 1))
                t_av = p.step("pe", mmav, waits=[t_ex] +
                              (p.wr("l2a_av") if kt == 0 else []))
                p.rd(f"l2a_P{kt % 2}", t_av)
                t_exps.append(t_av)
                if dbg is not None and h == 0 and qs == 0 and kt < 2:
                    dP = p.dma("pool", lambda e, kt=kt, pt=pt: e.dma_start(
                        out=dbg[f"P{kt}"], in_=pt), waits=[t_ex])
                    p.rd(f"l2a_P{kt % 2}", dP)

            t_rs = p.step("dve", lambda e: e.reciprocal(
                rsum, ps_av[64:65, :]), waits=[t_exps[-1]] + p.wr("l2a_rsum"))
            p.rd("l2a_av", t_rs)
            t_rb = p.step("pe", lambda e: e.matmul(
                ps_sc[0][0:64, :], cm.ones1[:, 0:64], rsum,
                start=True, stop=True),
                waits=[t_rs] + p.wr("prj0") + cm.ready)
            p.rd("l2a_rsum", t_rb)
            t_ave = p.step("act", lambda e: e.copy(out=avt, in_=ps_av[0:64, :]),
                           waits=[t_exps[-1]] + p.wr("l2a_avt"))
            p.rd("l2a_av", t_ave)
            t_ob = p.step("dve", lambda e: e.tensor_mul(
                obf, avt, ps_sc[0][0:64, :]),
                waits=[t_ave, t_rb] + p.wr("l2a_obf"))
            p.rd("prj0", t_ob)
            p.rd("l2a_avt", t_ob)
            t_st = p.dma("act", lambda e, h=h, qs=qs: e.dma_start(
                out=attnout[h * 64:(h + 1) * 64, qs * TT:(qs + 1) * TT],
                in_=obf), waits=[t_ob])
            p.rd("l2a_obf", t_st)

    p.barrier()
    t_wo = p.dma("pool", lambda e: e.dma_start(
        out=wo[:], in_=aps["L2a_woT"].rearrange("(c p) o -> p c o", p=128)))
    for tt in range(NTT):
        at, aw, ai = aot.acquire()
        t_ld = p.dma("sp", lambda e, tt=tt, at=at: e.dma_start(
            out=at, in_=x_ap(attnout, tt * TT, TT)), waits=aw)
        evs = []
        for oc in range(NCH):
            pb = p.prj[oc % 2]

            def mmo(e, oc=oc, pb=pb, at=at):
                for c in range(NCH):
                    e.matmul(pb[:], wo[:, c, oc * 128:(oc + 1) * 128],
                             at[:, c, :], start=(c == 0), stop=(c == NCH - 1))
            t_mm = p.step("pe", mmo,
                          waits=[t_ld, t_wo] + tw + p.wr(f"prj{oc % 2}"))

            def evo(e, oc=oc, pb=pb):
                e.activation(out=ytile[:, oc, :], in_=pb[:], func=AF.Identity,
                             bias=boc[:, oc:oc + 1])
            t_ev = p.step("act", evo,
                          waits=[t_mm] + p.wr(f"l2a_yt{oc}"))
            p.rd(f"prj{oc % 2}", t_ev)
            evs.append(t_ev)
        aot.read(ai, *evs)
        t_stx, ycons = emit_resid_out_tile(p, cm, ytile, evs[-1], x_in, x_out,
                                           tt)
        for oc in range(NCH):
            p.rd(f"l2a_yt{oc}", ycons[0])
            p.rd(f"l2a_yt{oc}", ycons[1])
            p.rd(f"l2a_yt{oc}", ycons[2])
    p.barrier()



def emit_L3_attn(p, cm, aps, x_in, x_out):
    QH, HD = 16, 64
    NKT = S // 128
    attnout = p.nc.dram_tensor("l3a_attnout", [D, S], BF16, kind="Internal").ap()
    Kdram = p.nc.dram_tensor("l3a_K", [D, S], BF16, kind="Internal").ap()
    xlnd = p.nc.dram_tensor("l3a_xln", [D, S], BF16, kind="Internal").ap()
    BRd = p.nc.dram_tensor("l3a_BR", [16, 8704], BF16, kind="Internal").ap()

    W1a = p.sb("l3a_W1a", [128, NCH, D], BF16)
    W1b = p.sb("l3a_W1b", [128, NCH, D], BF16)
    bkc = p.sb("l3a_bkc", [128, NCH], F32)
    bqc = p.sb("l3a_bqc", [128, NCH], F32)
    boc = p.sb("l3a_boc", [128, NCH], F32)
    bvr = p.sb("l3a_bvr", [1, D], BF16)
    hsplit = p.sb("l3a_hsplit", [128, 2], BF16)
    gruw2 = p.sb("l3a_gruw2", [128, 8], BF16)
    grub = p.sb("l3a_grub", [8, 1], F32)
    gruc = p.sb("l3a_gruc", [1, 16], F32)
    grp = p.sb("l3a_grp", [8, 2], F32)
    relw = p.sb("l3a_relw", [32, 16], BF16)
    ohp = Rot([p.sb(f"l3a_oh{i}", [32, TT], BF16) for i in range(2)])
    Vt = p.sb("l3a_V", [128, NKT, QH * 65], BF16)
    qsq = p.sb("l3a_qsq", [128, 1, TT], BF16)
    nrm = p.sb("l3a_nrm", [1, 2 * QH + 4], F32)
    runk = p.sb("l3a_runk", [2, NCH + 1], F32)
    kwr = p.sb("l3a_kwr", [128, TT], BF16)
    PBm = p.sb("l3a_PBm", [128, 8192], BF16)
    nsq = PBm[:].rearrange("p x -> p x").bitcast(F32)[0:2, 0:TT]
    tw = [
        p.dma("pool", lambda e: e.dma_start(
            out=W1a[:], in_=aps["L3a_Wk"].rearrange("(c p) o -> p c o", p=128))),
        p.dma("pool", lambda e: e.dma_start(
            out=W1b[:], in_=aps["L3a_Wv"].rearrange("(c p) o -> p c o", p=128))),
        p.dma("sp", lambda e: e.dma_start(out=bkc[:], in_=aps["L3a_bkc"])),
        p.dma("sp", lambda e: e.dma_start(out=bqc[:], in_=aps["L3a_bqc"])),
        p.dma("sp", lambda e: e.dma_start(out=boc[:], in_=aps["L3a_boc"])),
        p.dma("pool", lambda e: e.dma_start(out=bvr[:], in_=aps["L3a_bv"][None, :])),
        p.dma("pool", lambda e: e.dma_start(out=hsplit[:], in_=aps["hsplit"])),
        p.dma("pool", lambda e: e.dma_start(out=gruw2[:], in_=aps["L3a_gruw2"])),
        p.dma("sp", lambda e: e.dma_start(out=grub[:], in_=aps["L3a_grub"])),
        p.dma("sp", lambda e: e.dma_start(out=gruc[:], in_=aps["L3a_gruc"])),
        p.dma("sp", lambda e: e.dma_start(out=grp[:], in_=aps["grp8"])),
        p.dma("pool", lambda e: e.dma_start(out=relw[:], in_=aps["L3a_rel"])),
    ]
    t_bq = p.step("dve", lambda e: e.tensor_scalar_mul(bqc[:], bqc[:], 0.125),
                  waits=tw)
    _wvflat = W1b[:].rearrange("p c o -> p (c o)")
    _wvf32 = _wvflat.bitcast(F32)
    Pb = [_wvflat[:, 0:TT], _wvflat[:, TT:2 * TT]]
    avt = _wvf32[0:64, TT:2 * TT]
    rsum = _wvf32[0:1, 2 * TT:3 * TT]
    obf = _wvflat[0:64, 6 * TT:7 * TT]
    p.prj = [p.psum("l3a_prj0", [128, TT]), p.psum("l3a_prj1", [128, TT])]
    ps_av = p.psum("l3a_av", [65, TT])
    ps_n = ps_av[0:2, :]
    ps_v = p.psum("l3a_psv", [128, TT])

    t_ones = p.step("pool", lambda e: e.memset(
        Vt[:].rearrange("p t (h x) -> p t h x", h=QH)[:, :, :, 64:65], 1.0))
    t_nz = p.step("dve", lambda e: (e.memset(nrm[:], 0.0),
                                    e.memset(runk[:], 0.0)))

    for u0 in range(0, 8704, TT):
        oh, ohw, ohi = ohp.acquire()
        t_oh = p.dma("pool", lambda e, u0=u0, oh=oh: e.dma_start(
            out=oh[:], in_=aps["OH"][:, u0:u0 + TT]), waits=ohw)
        t_mm = p.step("pe", lambda e, oh=oh: e.matmul(
            ps_av[0:16, :], relw[:], oh[:], start=True, stop=True),
            waits=[t_oh] + tw + p.wr("l3a_av"))
        t_ev = p.step("act", lambda e: e.copy(out=kwr[0:16, :], in_=ps_av[0:16, :]),
                      waits=[t_mm] + p.wr("l3a_kwr"))
        p.rd("l3a_av", t_ev)
        t_st = p.dma("act", lambda e, u0=u0: e.dma_start(
            out=BRd[:, u0:u0 + TT], in_=kwr[0:16, :]), waits=[t_ev])
        p.rd("l3a_kwr", t_st)
        ohp.read(ohi, t_mm)

    kevs, vevs = [], []
    for tt in range(NTT):
        xl = cm.xbf
        t_xl, _ = emit_ln_in_tile(p, cm, x_in, tt, xl, [])
        lns = []
        for oc in range(NCH):
            pb = p.prj[oc % 2]

            def mmk(e, oc=oc, pb=pb, xl=xl):
                for c in range(NCH):
                    e.matmul(pb[:], W1a[:, c, oc * 128:(oc + 1) * 128],
                             xl[:, c, :], start=(c == 0), stop=(c == NCH - 1))
            t_mm = p.step("pe", mmk, waits=[t_xl] + tw + p.wr(f"prj{oc % 2}"))
            t_ev = p.step("act", lambda e, oc=oc, pb=pb: e.activation(
                out=kwr[:], in_=pb[:], func=AF.Identity,
                bias=bkc[:, oc:oc + 1]), waits=[t_mm] + p.wr("l3a_kwr"))
            p.rd(f"prj{oc % 2}", t_ev)
            t_st = p.dma("act", lambda e, oc=oc, tt=tt: e.dma_start(
                out=Kdram[oc * 128:(oc + 1) * 128, tt * TT:(tt + 1) * TT],
                in_=kwr[:]), waits=[t_ev])
            p.rd("l3a_kwr", t_st)
            kevs.append(t_st)
            t_sq = p.step("act", lambda e, oc=oc: e.activation(
                out=qsq[:, 0, :], in_=kwr[:], func=AF.Square),
                waits=[t_ev] + p.wr("l3a_qsq"))
            p.rd("l3a_kwr", t_sq)
            t_n = p.step("pe", lambda e: e.matmul(
                ps_n, hsplit[:], qsq[:, 0, :], start=True, stop=True),
                waits=[t_sq] + p.wr("l3a_psn"))
            p.rd("l3a_qsq", t_n)
            t_evn = p.step("act", lambda e: e.copy(out=nsq[:], in_=ps_n),
                           waits=[t_n] + p.wr("l3a_nsq"))
            p.rd("l3a_psn", t_evn)

            def mxn(e, oc=oc):
                e.reduce_max(runk[:, NCH:NCH + 1], nsq, axis=AX.X)
                e.drain()
                e.tensor_tensor(runk[:, oc:oc + 1], runk[:, oc:oc + 1],
                                runk[:, NCH:NCH + 1], op=ALU.max)
                e.drain()
            t_mx = p.step("dve", mxn, waits=[t_evn, t_nz])
            p.rd("l3a_nsq", t_mx)
            kevs.append(t_mx)
        for s_ in range(4):
            for vh in range(2):
                def mmv2(e, s_=s_, tt=tt, xl=xl, vh=vh):
                    for c in range(NCH):
                        e.matmul(ps_v[:, 0:TT], xl[:, c, s_ * 128:(s_ + 1) * 128],
                                 W1b[:, c, vh * TT:(vh + 1) * TT],
                                 start=(c == 0), stop=False)
                    e.matmul(ps_v[:, 0:TT], cm.ones1bf[:],
                             bvr[:, vh * TT:(vh + 1) * TT], start=False,
                             stop=True)
                t_mv = p.step("pe", mmv2,
                              waits=[t_xl] + tw + p.wr("l3a_psv"))

                def evv(e, s_=s_, tt=tt, vh=vh):
                    e.copy(out=Vt[:, tt * 4 + s_, :].rearrange(
                        "p (h x) -> p h x", h=QH)[:, vh * 8:(vh + 1) * 8, 0:64],
                        in_=ps_v[:, 0:TT].rearrange("p (h x) -> p h x", h=8))
                t_ev = p.step("act", evv, waits=[t_mv] + p.wr("l3a_Vs"))
                p.rd("l3a_psv", t_ev)
                vevs.append(t_ev)
        t_xst = p.dma("act", lambda e, tt=tt: e.dma_start(
            out=x_ap(xlnd, tt * TT, TT), in_=cm.xbf[:]), waits=[t_xl])
        for t in kevs[-16:] + vevs[-8:] + [t_xst]:
            p.rd("ln_src", t)

    t_drn = p.step("dve", lambda e: e.drain(), waits=kevs)
    t_mv = [
        p.dma("sp", lambda e: e.dma_start(
            out=nrm[0:1, QH:2 * QH].rearrange("a (o h) -> a o h", h=2)[:, :, 0:1],
            in_=runk[0:1, 0:NCH], allow_slow_non_contiguous=True),
            waits=[t_drn]),
        p.dma("sp", lambda e: e.dma_start(
            out=nrm[0:1, QH:2 * QH].rearrange("a (o h) -> a o h", h=2)[:, :, 1:2],
            in_=runk[1:2, 0:NCH], allow_slow_non_contiguous=True),
            waits=[t_drn]),
    ]
    barr = p.barrier()

    t_wq = p.dma("pool", lambda e: e.dma_start(
        out=W1a[:], in_=aps["L3a_Wq"].rearrange("(c p) o -> p c o", p=128)))
    Krot = Rot([p.sb(f"l3a_K{i}", [128, S], BF16) for i in range(2)])
    qtile = p.sb("l3a_qt", [128, TT], BF16)
    g8 = _wvf32[0:8, 1792:1792 + TT]
    gab = _wvf32[0:2, 2304:2304 + TT]
    gate = _wvf32[0:1, 2816:2816 + TT]
    gdiag = p.sb("l3a_gd", [128, TT], BF16)
    negc = p.sb("l3a_negc", [128, 1], F32)
    from concourse.ap import AP as _AP

    xlt = Rot([p.sb("l3a_xl0", [128, NCH, TT], BF16)])
    ps_n2 = ps_v[0:2, :]
    for h in range(QH):
        qc = h // 2
        qr = (h % 2) * 64
        kb, kbw, kbi = Krot.acquire()
        t_kl = p.dma("sp", lambda e, h=h, kb=kb, qr=qr: e.dma_start(
            out=kb[qr:qr + 64, :], in_=Kdram[h * 64:(h + 1) * 64, :]),
            waits=kbw + barr)
        def pbload(e, h=h):
            return [e.dma_start(out=PBm[pp:pp + 1, 0:8191],
                                in_=BRd[h:h + 1, 128 - pp:128 - pp + 8191])
                    for pp in range(128)]
        t_pb = p.dma("sp", pbload, waits=barr + p.wr("l3a_PBm"), n=128)
        for qs in range(NTT):
            xl, xlw, xli = xlt.acquire()
            t_xl = p.dma("sp", lambda e, qs=qs, xl=xl: e.dma_start(
                out=xl[:], in_=x_ap(xlnd, qs * TT, TT)), waits=xlw + barr)

            def mmq(e, xl=xl, h=h):
                for c in range(NCH):
                    e.matmul(p.prj[0][0:64, :], W1a[:, c, h * 64:(h + 1) * 64],
                             xl[:, c, :], start=(c == 0), stop=(c == NCH - 1))
            t_qm = p.step("pe", mmq, waits=[t_xl, t_wq, t_bq] + p.wr("prj0"))

            def evq(e, qr=qr, qc=qc):
                e.activation(out=qtile[qr:qr + 64, :], in_=p.prj[0][0:64, :],
                             func=AF.Identity, scale=0.125,
                             bias=bqc[qr:qr + 64, qc:qc + 1])
            t_qe = p.step("act", evq, waits=[t_qm] + p.wr("l3a_qt"))
            p.rd("prj0", t_qe)

            t_sq = p.step("act", lambda e: e.activation(
                out=qsq[:, 0, :], in_=qtile[:], func=AF.Square),
                waits=[t_qe] + p.wr("l3a_qsq"))
            t_n = p.step("pe", lambda e, qr=qr: e.matmul(
                ps_n2[0:1, :], hsplit[:, qr // 64:qr // 64 + 1],
                qsq[:, 0, :], start=True, stop=True),
                waits=[t_sq] + p.wr("l3a_psn2"))
            p.rd("l3a_qsq", t_n)

            def mkM(e, h=h):
                e.reduce_max(nrm[:, 2 * QH:2 * QH + 1], ps_n2[0:1, :], axis=AX.X)
                e.drain()
                e.tensor_mul(nrm[:, 2 * QH:2 * QH + 1],
                             nrm[:, 2 * QH:2 * QH + 1],
                             nrm[:, QH + h:QH + h + 1])
                e.drain()
            t_m1 = p.step("dve", mkM, waits=[t_n] + t_mv + p.wr("l3a_nrmM"))
            p.rd("l3a_psn2", t_m1)
            t_m2 = p.step("act", lambda e: e.activation(
                out=nrm[:, 2 * QH + 1:2 * QH + 2],
                in_=nrm[:, 2 * QH:2 * QH + 1], func=AF.Sqrt), waits=[t_m1])
            t_m3 = p.step("dve", lambda e: e.tensor_scalar_mul(
                nrm[:, 2 * QH + 1:2 * QH + 2],
                nrm[:, 2 * QH + 1:2 * QH + 2], -1.0), waits=[t_m2])
            p.rd("l3a_nrmM", t_m3)
            t_m4 = p.step("pe", lambda e: e.matmul(
                p.prj[0][:, 0:1], cm.ones1[:], nrm[:, 2 * QH + 1:2 * QH + 2],
                start=True, stop=True), waits=[t_m3] + p.wr("prj0") + cm.ready)
            t_M = p.step("dve", lambda e: e.tensor_copy(
                negc[:], p.prj[0][:, 0:1]), waits=[t_m4] + p.wr("l3a_negc"))
            p.rd("prj0", t_M)

            def mmg(e, qr=qr, qc=qc, xl=xl):
                e.matmul(p.prj[1][0:8, :], gruw2[qr:qr + 64, :],
                         xl[qr:qr + 64, qc, :], start=True, stop=True)
            t_g1 = p.step("pe", mmg, waits=[t_xl] + tw + p.wr("prj1"))
            t_g2 = p.step("act", lambda e: e.activation(
                out=g8, in_=p.prj[1][0:8, :], func=AF.Identity,
                bias=grub[:, 0:1]), waits=[t_g1] + p.wr("l3a_g8"))
            p.rd("prj1", t_g2)
            t_g3 = p.step("pe", lambda e: e.matmul(
                p.prj[1][0:1, 0:TT], grp[:, 0:1], g8, start=True, stop=True),
                waits=[t_g2] + p.wr("prj1"))
            t_g4a = p.step("act", lambda e: e.activation(
                out=gate, in_=p.prj[1][0:1, :], func=AF.Sigmoid),
                waits=[t_g3] + p.wr("l3a_gate"))
            p.rd("prj1", t_g4a)
            t_g3b = p.step("pe", lambda e: e.matmul(
                p.prj[1][0:1, 0:TT], grp[:, 1:2], g8, start=True, stop=True),
                waits=[t_g4a] + p.wr("prj1"))
            p.rd("l3a_g8", t_g3b)
            t_g4b = p.step("act", lambda e: e.activation(
                out=gab[0:1, :], in_=p.prj[1][0:1, :], func=AF.Sigmoid),
                waits=[t_g3b] + p.wr("l3a_gab"))
            p.rd("prj1", t_g4b)

            def gfin(e, h=h):
                e.tensor_scalar(out=gab[0:1, :], in0=gab[0:1, :],
                                scalar1=gruc[:, h:h + 1], scalar2=-1.0,
                                op0=ALU.mult, op1=ALU.add)
                e.drain()
                e.tensor_mul(gate, gate, gab[0:1, :])
                e.drain()
                e.tensor_scalar_add(gate, gate, 2.0)
                e.drain()
            t_g5 = p.step("dve", gfin, waits=[t_g4a, t_g4b] + p.wr("l3a_gate2"))
            p.rd("l3a_gate", t_g5)
            p.rd("l3a_gab", t_g5)

            gevs = []
            for qsub in range(4):
                t_gb = p.step("pe", lambda e, qsub=qsub: e.matmul(
                    p.prj[1][:, 0:128], cm.ones1[:],
                    gate[:, qsub * 128:(qsub + 1) * 128], start=True,
                    stop=True), waits=[t_g5] + p.wr("prj1") + cm.ready)
                p.rd("l3a_gate2", t_gb)
                t_gd = p.step("dve", lambda e, qsub=qsub: e.tensor_mul(
                    gdiag[:, qsub * 128:(qsub + 1) * 128], p.prj[1][:, 0:128],
                    cm.identbf[:]), waits=[t_gb] + p.wr(f"l3a_gd{qsub}"))
                p.rd("prj1", t_gd)
                gevs.append(t_gd)

            t_exps = []
            for kt in range(NKT):
                pb = p.prj[kt % 2]
                pt = Pb[kt % 2]

                def mmsc(e, pb=pb, kt=kt, qr=qr, qs=qs, kb=kb):
                    e.matmul(pb[:], kb[qr:qr + 64, kt * 128:(kt + 1) * 128],
                             qtile[qr:qr + 64, :], start=True, stop=False)
                    for qsub in range(4):
                        c0 = kt * 128 - (qs * 512 + qsub * 128) + 4095
                        e.matmul(pb[:, qsub * 128:(qsub + 1) * 128],
                                 PBm[:, c0:c0 + 128],
                                 gdiag[:, qsub * 128:(qsub + 1) * 128],
                                 start=False, stop=(qsub == 3))
                t_sc = p.step("pe", mmsc,
                              waits=[t_qe, t_kl, t_pb, t_M] + gevs +
                              p.wr(f"prj{kt % 2}"))
                for qsub in range(4):
                    p.rd(f"l3a_gd{qsub}", t_sc)
                p.rd("l3a_qt", t_sc)
                p.rd("l3a_PBm", t_sc)
                Krot.read(kbi, t_sc)

                def expev(e, pb=pb, pt=pt):
                    e.activation(out=pt[:], in_=pb[:], func=AF.Exp,
                                 bias=negc[:, 0:1])
                t_ex = p.step("act", expev,
                              waits=[t_sc, t_M] + p.wr(f"l3a_P{kt % 2}"))
                p.rd(f"prj{kt % 2}", t_ex)
                p.rd("l3a_negc", t_ex)

                def mmav(e, pt=pt, kt=kt, h=h):
                    e.matmul(ps_av[:], Vt[:, kt, h * 65:h * 65 + 65],
                             pt[:], start=(kt == 0), stop=(kt == NKT - 1))
                t_av = p.step("pe", mmav, waits=[t_ex] +
                              (p.wr("l3a_av") if kt == 0 else []))
                p.rd(f"l3a_P{kt % 2}", t_av)
                t_exps.append(t_av)

            t_rs = p.step("dve", lambda e: e.reciprocal(
                rsum, ps_av[64:65, :]), waits=[t_exps[-1]] + p.wr("l3a_rsum"))
            p.rd("l3a_av", t_rs)
            t_rb = p.step("pe", lambda e: e.matmul(
                p.prj[0][0:64, :], cm.ones1[:, 0:64], rsum,
                start=True, stop=True),
                waits=[t_rs] + p.wr("prj0") + cm.ready)
            p.rd("l3a_rsum", t_rb)
            t_ave = p.step("act", lambda e: e.copy(out=avt, in_=ps_av[0:64, :]),
                           waits=[t_exps[-1]] + p.wr("l3a_avt"))
            p.rd("l3a_av", t_ave)
            t_ob = p.step("dve", lambda e: e.tensor_mul(
                obf, avt, p.prj[0][0:64, :]),
                waits=[t_ave, t_rb] + p.wr("l3a_obf"))
            p.rd("prj0", t_ob)
            p.rd("l3a_avt", t_ob)
            t_st = p.dma("act", lambda e, h=h, qs=qs: e.dma_start(
                out=attnout[h * 64:(h + 1) * 64, qs * TT:(qs + 1) * TT],
                in_=obf), waits=[t_ob])
            p.rd("l3a_obf", t_st)
            xlt.read(xli, t_qm, t_g1)
    p.barrier()
    t_wo = p.dma("pool", lambda e: e.dma_start(
        out=W1a[:], in_=aps["L3a_Wo"].rearrange("(c p) o -> p c o", p=128)))
    _vflat = Vt[:].rearrange("p t x -> p (t x)")
    ytile = _vflat.bitcast(F32)[:, 0:4096].rearrange("p (c t) -> p c t", c=NCH)
    aot = Rot([_vflat[:, 8192 + i * 4096:8192 + (i + 1) * 4096]
               .rearrange("p (c t) -> p c t", c=NCH) for i in range(2)])
    for tt in range(NTT):
        at, aw, ai = aot.acquire()
        t_ld = p.dma("sp", lambda e, tt=tt, at=at: e.dma_start(
            out=at[:], in_=x_ap(attnout, tt * TT, TT)), waits=aw)
        evs = []
        for oc in range(NCH):
            pb = p.prj[oc % 2]

            def mmo(e, oc=oc, pb=pb, at=at):
                for c in range(NCH):
                    e.matmul(pb[:], W1a[:, c, oc * 128:(oc + 1) * 128],
                             at[:, c, :], start=(c == 0), stop=(c == NCH - 1))
            t_mm = p.step("pe", mmo,
                          waits=[t_ld, t_wo] + p.wr(f"prj{oc % 2}"))

            def evo(e, oc=oc, pb=pb):
                e.activation(out=ytile[:, oc, :], in_=pb[:], func=AF.Identity,
                             bias=boc[:, oc:oc + 1])
            t_ev = p.step("act", evo, waits=[t_mm] + p.wr(f"l3a_yt{oc}"))
            p.rd(f"prj{oc % 2}", t_ev)
            evs.append(t_ev)
        aot.read(ai, *evs)
        t_stx, ycons = emit_resid_out_tile(p, cm, ytile, evs[-1], x_in, x_out,
                                           tt)
        for oc in range(NCH):
            for t in ycons:
                p.rd(f"l3a_yt{oc}", t)
    p.barrier()



def emit_L1_attn(p, cm, aps, x_in, x_out):
    DI = 2048
    NDC = DI // 128
    TC = 1024
    NTC = S // TC
    xcd = p.nc.dram_tensor("l1a_xc", [DI, S], F32, kind="Internal").ap()
    xcbd = p.nc.dram_tensor("l1a_xcb", [DI, S], BF16, kind="Internal").ap()
    dtd = p.nc.dram_tensor("l1a_dt", [DI, S], F32, kind="Internal").ap()
    zd = p.nc.dram_tensor("l1a_z", [DI, S], BF16, kind="Internal").ap()
    gd = p.nc.dram_tensor("l1a_g", [DI, S], BF16, kind="Internal").ap()

    wbig = p.sb("l1a_wbig", [128, 16, DI], BF16)
    wxp = p.sb("l1a_wxp", [128, 16, 32], BF16)
    binc = p.sb("l1a_binc", [128, 32], F32)
    bxB = p.sb("l1a_bxB", [16, 1], F32)
    bxC = p.sb("l1a_bxC", [16, 1], F32)
    bdtc = p.sb("l1a_bdtc", [128, 16], F32)
    boutc = p.sb("l1a_boutc", [128, NCH], F32)
    Acol = p.sb("l1a_Acol", [128, 256], F32)
    Dcol = p.sb("l1a_Dcol", [128, 16], F32)
    Bd = p.nc.dram_tensor("l1a_B", [16, S], F32, kind="Internal").ap()
    Cd = p.nc.dram_tensor("l1a_C", [16, S], F32, kind="Internal").ap()
    Brow = p.sb("l1a_Brow", [16, TT], F32)
    Bst = p.sb("l1a_Bst", [1, TC], F32)
    Cst = p.sb("l1a_Cst", [1, TC], F32)
    hcar = p.sb("l1a_hcar", [128, 256], F32)
    sc1 = Rot([p.sb(f"l1a_sc1{i}", [128, TT], F32) for i in range(2)])
    sc2 = Rot([p.sb(f"l1a_sc2{i}", [128, TT], F32) for i in range(2)])
    xcb = p.sb("l1a_xcbt", [128, 16, TT], BF16)
    p.prj = [p.psum("l1a_prj0", [128, TT]), p.psum("l1a_prj1", [128, TT])]
    ps_x = p.psum("l1a_psx", [16, TT])

    win_v = wbig[:].rearrange("p c f -> p (c f)").rearrange(
        "p (c f) -> p c f", c=8)
    t_win = p.dma("pool", lambda e: e.dma_start(
        out=win_v, in_=aps["L1a_Win"].rearrange("(c p) f -> p c f", p=128)))
    tw = [
        p.dma("pool", lambda e: e.dma_start(
            out=wxp[:], in_=aps["L1a_Wxp"].rearrange("(c p) f -> p c f", p=128))),
        p.dma("sp", lambda e: e.dma_start(out=binc[:], in_=aps["L1a_binc"])),
        p.dma("sp", lambda e: e.dma_start(out=bxB[:], in_=aps["L1a_bxB"])),
        p.dma("sp", lambda e: e.dma_start(out=bxC[:], in_=aps["L1a_bxC"])),
        p.dma("sp", lambda e: e.dma_start(out=bdtc[:], in_=aps["L1a_bdtc"])),
        p.dma("sp", lambda e: e.dma_start(out=boutc[:], in_=aps["L1a_boutc"])),
        p.dma("sp", lambda e: e.dma_start(out=Acol[:], in_=aps["L1a_Alog"])),
        p.dma("sp", lambda e: e.dma_start(out=Dcol[:], in_=aps["L1a_Dcol"])),
    ]
    t_A1 = p.step("act", lambda e: e.activation(
        out=Acol[:], in_=Acol[:], func=AF.Exp), waits=tw)
    t_A = p.step("dve", lambda e: e.tensor_scalar_mul(Acol[:], Acol[:], -1.0),
                 waits=[t_A1])
    t_hz = p.step("dve", lambda e: e.memset(hcar[:], 0.0))

    for tt in range(NTT):
        xl = cm.xbf
        t_xl, _ = emit_ln_in_tile(p, cm, x_in, tt, xl, [])
        rds = []
        for oc in range(32):
            pb = p.prj[oc % 2]

            def mm(e, oc=oc, pb=pb, xl=xl):
                for c in range(NCH):
                    e.matmul(pb[:], win_v[:, c, oc * 128:(oc + 1) * 128],
                             xl[:, c, :], start=(c == 0), stop=(c == NCH - 1))
            t_mm = p.step("pe", mm,
                          waits=[t_xl, t_win] + tw + p.wr(f"prj{oc % 2}"))
            rds.append(t_mm)
            s1, s1w, s1i = sc1.acquire()
            if oc < 16:
                t_ev = p.step("act", lambda e, oc=oc, pb=pb, s1=s1:
                              e.activation(out=s1[:], in_=pb[:], func=AF.Silu,
                                           bias=binc[:, oc:oc + 1]),
                              waits=[t_mm] + s1w + tw)
                p.rd(f"prj{oc % 2}", t_ev)
                d1 = p.dma("sp", lambda e, oc=oc, tt=tt, s1=s1: e.dma_start(
                    out=xcd[oc * 128:(oc + 1) * 128, tt * TT:(tt + 1) * TT],
                    in_=s1[:]), waits=[t_ev])
                d2 = p.dma("pool", lambda e, oc=oc, tt=tt, s1=s1: e.dma_start(
                    out=xcbd[oc * 128:(oc + 1) * 128, tt * TT:(tt + 1) * TT],
                    in_=s1[:]), waits=[t_ev])
                sc1.read(s1i, d1, d2)
            else:
                t_ev = p.step("act", lambda e, oc=oc, pb=pb, s1=s1:
                              e.activation(out=s1[:], in_=pb[:],
                                           func=AF.Identity,
                                           bias=binc[:, oc:oc + 1]),
                              waits=[t_mm] + s1w + tw)
                p.rd(f"prj{oc % 2}", t_ev)
                d2 = p.dma("pool", lambda e, oc=oc, tt=tt, s1=s1: e.dma_start(
                    out=zd[(oc - 16) * 128:(oc - 15) * 128,
                           tt * TT:(tt + 1) * TT],
                    in_=s1[:]), waits=[t_ev])
                sc1.read(s1i, d2)
        for t in rds:
            p.rd("ln_src", t)
    p.barrier()

    t_wdt = p.dma("pool", lambda e: e.dma_start(
        out=wbig[:], in_=aps["L1a_Wdt"].rearrange("(c p) f -> p c f", p=128)))
    for tt in range(NTT):
        t_xc = p.dma("sp", lambda e, tt=tt: e.dma_start(
            out=xcb[:], in_=xcbd.rearrange("(c p) t -> p c t", p=128)
            [:, :, tt * TT:(tt + 1) * TT]), waits=p.wr("l1a_xcb"))
        for oc in range(16):
            pb = p.prj[oc % 2]

            def mm(e, oc=oc, pb=pb):
                for c in range(16):
                    e.matmul(pb[:], wbig[:, c, oc * 128:(oc + 1) * 128],
                             xcb[:, c, :], start=(c == 0), stop=(c == 15))
            t_mm = p.step("pe", mm,
                          waits=[t_xc, t_wdt] + p.wr(f"prj{oc % 2}"))
            p.rd("l1a_xcb", t_mm)
            s1, s1w, s1i = sc1.acquire()
            s2, s2w, s2i = sc2.acquire()

            def sp_a(e, oc=oc, pb=pb, s1=s1, s2=s2):
                e.activation(out=s2[:], in_=pb[:], func=AF.Abs,
                             bias=bdtc[:, oc:oc + 1])
                e.activation(out=s2[:], in_=s2[:], func=AF.Exp, scale=-1.0)
                e.activation(out=s2[:], in_=s2[:], func=AF.Ln, bias=1.0)
                e.activation(out=s1[:], in_=pb[:], func=AF.Relu,
                             bias=bdtc[:, oc:oc + 1])
            t_sp = p.step("act", sp_a, waits=[t_mm] + s1w + s2w + tw)
            p.rd(f"prj{oc % 2}", t_sp)
            t_dt = p.step("dve", lambda e, s1=s1, s2=s2: e.tensor_add(
                s1[:], s1[:], s2[:]), waits=[t_sp])
            sc2.read(s2i, t_dt)
            d1 = p.dma("sp", lambda e, oc=oc, tt=tt, s1=s1: e.dma_start(
                out=dtd[oc * 128:(oc + 1) * 128, tt * TT:(tt + 1) * TT],
                in_=s1[:]), waits=[t_dt])
            sc1.read(s1i, d1)
        for which, bdram, bcol in (("B", Bd, bxB), ("C", Cd, bxC)):
            off = 0 if which == "B" else 16

            def mmx(e, off=off):
                for c in range(16):
                    e.matmul(ps_x[:], wxp[:, c, off:off + 16], xcb[:, c, :],
                             start=(c == 0), stop=(c == 15))
            t_mx = p.step("pe", mmx, waits=[t_xc] + tw + p.wr("l1a_psx"))
            p.rd("l1a_xcb", t_mx)
            t_ex = p.step("act", lambda e, bcol=bcol: e.activation(
                out=Brow[:], in_=ps_x[:], func=AF.Identity,
                bias=bcol[:, 0:1]), waits=[t_mx] + tw + p.wr("l1a_brow"))
            p.rd("l1a_psx", t_ex)
            d1 = p.dma("act", lambda e, bdram=bdram, tt=tt: e.dma_start(
                out=bdram[:, tt * TT:(tt + 1) * TT], in_=Brow[:]),
                waits=[t_ex])
            p.rd("l1a_brow", d1)
    p.barrier()

    _wf = wbig[:].rearrange("p c f -> p (c f)")
    _wf32 = _wf.bitcast(F32)
    dtt = _wf32[:, 0:4096].rearrange("p (j t) -> p j t", j=4)
    ut = _wf32[:, 4096:8192].rearrange("p (j t) -> p j t", j=4)
    dtu = _wf32[:, 8192:12288].rearrange("p (j t) -> p j t", j=4)
    yac = _wf32[:, 12288:16384].rearrange("p (j t) -> p j t", j=4)
    zt = p.sb("l1a_zt", [128, 4, TC], BF16)
    Bbc = p.sb("l1a_Bbc", [128, TC], F32)
    Cbc = p.sb("l1a_Cbc", [128, TC], F32)
    _xcf = xcb[:].rearrange("p c t -> p (c t)").bitcast(F32)
    dA = Rot([_xcf[:, 0:TC]])
    dBu = Rot([_xcf[:, TC:2 * TC]])
    ht = Rot([_xcf[:, 2 * TC:3 * TC], _xcf[:, 3 * TC:4 * TC]])
    gt = Rot([p.sb(f"l1a_gt{i}", [128, TC], BF16) for i in range(2)])
    szs = p.sb("l1a_szs", [128, TC], F32)

    for tc in range(NTC):
        for dcg in range(4):
            t_ld = p.dma("sp", lambda e, tc=tc, dcg=dcg: [
                e.dma_start(out=dtt, in_=dtd.rearrange(
                    "(g c p) t -> p g c t", p=128, g=4)
                    [:, dcg, :, tc * TC:(tc + 1) * TC].rearrange(
                        "p c t -> p c t")),
                e.dma_start(out=ut, in_=xcd.rearrange(
                    "(g c p) t -> p g c t", p=128, g=4)
                    [:, dcg, :, tc * TC:(tc + 1) * TC]),
                e.dma_start(out=zt[:], in_=zd.rearrange(
                    "(g c p) t -> p g c t", p=128, g=4)
                    [:, dcg, :, tc * TC:(tc + 1) * TC]),
            ], waits=p.wr("l1a_phB") + tw, n=3)

            t_dtu = p.step("pool", lambda e: e.tensor_mul(dtu, dtt, ut),
                           waits=[t_ld] + p.wr("l1a_dtu"))
            p.rd("l1a_phB", t_dtu)
            yins = []
            for j in range(4):
                t_yi = p.step("dve", lambda e, j=j, dcg=dcg: e.tensor_scalar(
                    out=yac[:, j, :], in0=ut[:, j, :],
                    scalar1=Dcol[:, dcg * 4 + j:dcg * 4 + j + 1],
                    scalar2=None, op0=ALU.mult),
                    waits=[t_ld, t_A] + p.wr(f"l1a_yac{j}"))
                p.rd("l1a_phB", t_yi)
                yins.append(t_yi)

            for s in range(16):
                t_bl = p.dma("sp", lambda e, s=s, tc=tc: [
                    e.dma_start(out=Bst[:], in_=Bd[s:s + 1, tc * TC:(tc + 1) * TC]),
                    e.dma_start(out=Cst[:], in_=Cd[s:s + 1, tc * TC:(tc + 1) * TC]),
                ], waits=p.wr("l1a_bst"), n=2)
                t_b1 = p.step("pe", lambda e: (
                    e.matmul(p.prj[0][:], cm.ones1[:], Bst[:, 0:TT],
                             start=True, stop=True),
                    e.matmul(p.prj[1][:], cm.ones1[:], Bst[:, TT:TC],
                             start=True, stop=True)),
                    waits=[t_bl] + cm.ready + p.wr("prj0") + p.wr("prj1"))
                t_be = p.step("act", lambda e: (
                    e.copy(out=Bbc[:, 0:TT], in_=p.prj[0][:]),
                    e.copy(out=Bbc[:, TT:TC], in_=p.prj[1][:])),
                    waits=[t_b1] + p.wr("l1a_Bbc"))
                p.rd("prj0", t_be)
                p.rd("prj1", t_be)
                t_c1 = p.step("pe", lambda e: (
                    e.matmul(p.prj[0][:], cm.ones1[:], Cst[:, 0:TT],
                             start=True, stop=True),
                    e.matmul(p.prj[1][:], cm.ones1[:], Cst[:, TT:TC],
                             start=True, stop=True)),
                    waits=[t_be, t_bl] + cm.ready + p.wr("prj0") + p.wr("prj1"))
                p.rd("l1a_bst", t_c1)
                p.rd("l1a_bst", t_b1)
                t_ce = p.step("act", lambda e: (
                    e.copy(out=Cbc[:, 0:TT], in_=p.prj[0][:]),
                    e.copy(out=Cbc[:, TT:TC], in_=p.prj[1][:])),
                    waits=[t_c1] + p.wr("l1a_Cbc"))
                p.rd("prj0", t_ce)
                p.rd("prj1", t_ce)

                for j in range(4):
                    col = (dcg * 4 + j) * 16 + s
                    da, daw, dai = dA.acquire()
                    t_da = p.step("act", lambda e, j=j, col=col, da=da:
                                  e.activation(out=da, in_=dtt[:, j, :],
                                               func=AF.Exp,
                                               scale=Acol[:, col:col + 1]),
                                  waits=[t_ld, t_A] + daw)
                    db, dbw, dbi = dBu.acquire()
                    t_db = p.step("pool", lambda e, j=j, db=db: e.tensor_mul(
                        db, dtu[:, j, :], Bbc[:]),
                        waits=[t_dtu, t_be] + dbw)
                    p.rd("l1a_Bbc", t_db)
                    hh, hhw, hhi = ht.acquire()
                    t_h = p.step("dve", lambda e, col=col, da=da, db=db, hh=hh:
                                 e.tensor_tensor_scan(
                                     out=hh, data0=da, data1=db,
                                     initial=hcar[:, col:col + 1],
                                     op0=ALU.mult, op1=ALU.add),
                                 waits=[t_da, t_db, t_hz] + hhw +
                                 p.wr(f"l1a_hcar{col % 8}"))
                    dA.read(dai, t_h)
                    dBu.read(dbi, t_h)
                    t_hs = p.step("dve", lambda e, col=col, hh=hh:
                                  e.tensor_copy(hcar[:, col:col + 1],
                                                hh[:, TC - 1:TC]),
                                  waits=[t_h])
                    p.rd(f"l1a_hcar{col % 8}", t_hs)
                    t_hc = p.step("pool", lambda e, hh=hh: e.tensor_mul(
                        szs[:], hh, Cbc[:]),
                        waits=[t_h, t_ce] + p.wr("l1a_szs"))
                    p.rd("l1a_Cbc", t_hc)
                    ht.read(hhi, t_hc)
                    t_ya = p.step("dve", lambda e, j=j: e.tensor_add(
                        yac[:, j, :], yac[:, j, :], szs[:]),
                        waits=[t_hc, yins[j]])
                    p.rd("l1a_szs", t_ya)
                    p.rd(f"l1a_yac{j}", t_ya)
                p.rd("l1a_bc2", t_ce)
            for j in range(4):
                t_sz = p.step("act", lambda e, j=j: e.activation(
                    out=szs[:], in_=zt[:, j, :], func=AF.Silu),
                    waits=[t_ld] + p.wr("l1a_szs"))
                g, gw, gi = gt.acquire()
                t_g = p.step("dve", lambda e, j=j, g=g: e.tensor_mul(
                    g[:], yac[:, j, :], szs[:]), waits=[t_sz] + gw)
                p.rd("l1a_szs", t_g)
                p.rd(f"l1a_yac{j}", t_g)
                p.rd("l1a_phB", t_g)
                d1 = p.dma("pool", lambda e, j=j, tc=tc, dcg=dcg, g=g:
                           e.dma_start(out=gd[(dcg * 4 + j) * 128:
                                              (dcg * 4 + j + 1) * 128,
                                              tc * TC:(tc + 1) * TC],
                                       in_=g[:]), waits=[t_g])
                gt.read(gi, d1)
    p.barrier()

    t_wo = p.dma("pool", lambda e: e.dma_start(
        out=wbig[:, :, 0:1024],
        in_=aps["L1a_Wout"].rearrange("(c p) o -> p c o", p=128)))
    _wf2 = wbig[:].rearrange("p c f -> p (c f)").bitcast(F32)
    ytile = _wf2.rearrange("p (c x) -> p c x", c=16)[:, 0:8, 512:1024]
    for tt in range(NTT):
        t_g = p.dma("sp", lambda e, tt=tt: e.dma_start(
            out=xcb[:], in_=gd.rearrange("(c p) t -> p c t", p=128)
            [:, :, tt * TT:(tt + 1) * TT]), waits=p.wr("l1a_xcb"))
        evs = []
        for oc in range(NCH):
            pb = p.prj[oc % 2]

            def mmo(e, oc=oc, pb=pb):
                for c in range(16):
                    e.matmul(pb[:], wbig[:, c, oc * 128:(oc + 1) * 128],
                             xcb[:, c, :], start=(c == 0), stop=(c == 15))
            t_mm = p.step("pe", mmo,
                          waits=[t_g, t_wo] + p.wr(f"prj{oc % 2}"))
            p.rd("l1a_xcb", t_mm)

            def evo(e, oc=oc, pb=pb):
                e.activation(out=ytile[:, oc, :], in_=pb[:], func=AF.Identity,
                             bias=boutc[:, oc:oc + 1])
            t_ev = p.step("act", evo,
                          waits=[t_mm] + tw + p.wr(f"l1a_yt{oc}"))
            p.rd(f"prj{oc % 2}", t_ev)
            evs.append(t_ev)
        t_stx, ycons = emit_resid_out_tile(p, cm, ytile, evs[-1], x_in, x_out,
                                           tt)
        for oc in range(NCH):
            for t in ycons:
                p.rd(f"l1a_yt{oc}", t)
    p.barrier()



def emit_L0_attn(p, cm, aps, x_in, x_out, stages=(1, 2, 3), dbg=None):
    QH, HD, W = 16, 64, 16
    NKT = S // 128
    attnout = p.nc.dram_tensor("l0a_attnout", [D, S], BF16, kind="Internal").ap()

    wqkv = p.sb("l0a_wqkv", [128, NCH, 3072], BF16)
    bqkc = p.sb("l0a_bqkc", [128, 24], F32)
    cosb = p.sb("l0a_cos", [128, 2, TT], BF16)
    sinb = p.sb("l0a_sin", [128, 2, TT], BF16)
    hsplit = p.sb("l0a_hsplit", [128, 2], BF16)
    onescf = p.sb("l0a_onescf", [128, 1], F32)
    bvr0 = p.sb("l0a_bvr", [1, D], BF16)
    wmaskp = p.sb("l0a_wmp", [128, 128], BF16)
    wmaskc = p.sb("l0a_wmc", [128, 128], BF16)
    At = p.sb("l0a_A", [128, NCH, S], BF16)
    Kroll = p.sb("l0a_Kroll", [128, 8, 5, 128], BF16)
    Vroll = p.sb("l0a_Vroll", [128, 5, QH * 65], BF16)
    qrope = p.sb("l0a_qrope", [128, TT], BF16)
    rsc1 = p.sb("l0a_rsc1", [128, TT], F32)
    rsc2 = p.sb("l0a_rsc2", [128, TT], F32)
    qbf = p.sb("l0a_qbf", [128, TT], BF16)
    kbf = p.sb("l0a_kbf", [128, TT], BF16)
    sqs = p.sb("l0a_sqs", [128, TT], BF16)
    mq2 = p.sb("l0a_mq2", [2, 28], F32)
    negc = p.sb("l0a_negc", [128, 8], F32)
    Pw = p.sb("l0a_Pw", [128, TT], BF16)
    avw = rsc2[0:64, 0:128]
    rsw = rsc2[0:1, 128:256]

    p.prj = [p.psum("l0a_prj0", [128, TT]), p.psum("l0a_prj1", [128, TT])]
    ps_n = p.psum("l0a_psn", [2, TT])
    ps_av = p.psum("l0a_av", [65, TT])

    tw = [
        p.dma("pool", lambda e: e.dma_start(
            out=wqkv[:], in_=aps["L0a_WqkvT"].rearrange("(c p) o -> p c o", p=128))),
        p.dma("sp", lambda e: e.dma_start(out=bqkc[:], in_=aps["L0a_bqkvc"])),

        p.dma("pool", lambda e: e.dma_start(out=hsplit[:], in_=aps["hsplit"])),
        p.dma("sp", lambda e: e.dma_start(out=onescf[:], in_=aps["onescol"])),
        p.dma("pool", lambda e: e.dma_start(out=bvr0[:], in_=aps["L0a_bv"][None, :])),
    ]
    t_wm = p.dma("pool", lambda e: [
        e.dma_start(out=wmaskp[:], in_=aps["L0a_wmp"]),
        e.dma_start(out=wmaskc[:], in_=aps["L0a_wmc"]),
    ], n=2)
    t_bq = p.step("dve", lambda e: e.tensor_scalar_mul(
        bqkc[:, 0:8], bqkc[:, 0:8], 0.125), waits=tw)
    t_vz = p.step("pool", lambda e: (e.memset(Vroll[:], 0.0),
                                     e.memset(Kroll[:], 0.0)))
    t_vo = p.step("pool", lambda e: e.memset(
        Vroll[:].rearrange("p s (h x) -> p s h x", h=QH)[:, :, :, 64:65], 1.0),
        waits=[t_vz])
    t_mz = p.step("dve", lambda e: e.memset(mq2[:], 0.0))
    shuf = list(range(32))
    for i in range(8):
        shuf[i] = 8 + i
        shuf[8 + i] = i

    prevk = [t_mz]
    for tt in range(NTT):
        xl = cm.xbf
        t_xl, _ = emit_ln_in_tile(p, cm, x_in, tt, xl, [])
        lreads = []
        ci = tt % 2
        t_tab = p.dma("pool", lambda e, tt=tt, ci=ci: [
            e.dma_start(out=cosb[:, ci, :],
                        in_=aps["L0a_cos"][:, tt * TT:(tt + 1) * TT]),
            e.dma_start(out=sinb[:, ci, :],
                        in_=aps["L0a_sin"][:, tt * TT:(tt + 1) * TT]),
        ], waits=p.wr(f"l0a_tab{ci}"), n=2)
        for hp in range(NCH):
            def mmq(e, hp=hp, xl=xl):
                for c in range(NCH):
                    e.matmul(p.prj[0][:], wqkv[:, c, hp * 128:(hp + 1) * 128],
                             xl[:, c, :], start=(c == 0), stop=(c == NCH - 1))
            t_mm = p.step("pe", mmq, waits=[t_xl, t_bq] + tw + p.wr("prj0"))
            lreads.append(t_mm)
            t_qe = p.step("act", lambda e, hp=hp: e.activation(
                out=qbf[:], in_=p.prj[0][:], func=AF.Identity, scale=0.125,
                bias=bqkc[:, hp:hp + 1]), waits=[t_mm] + p.wr("l0a_qbf"))
            p.rd("prj0", t_qe)
            def mmk(e, hp=hp, xl=xl):
                for c in range(NCH):
                    e.matmul(p.prj[1][:],
                             wqkv[:, c, 1024 + hp * 128:1024 + (hp + 1) * 128],
                             xl[:, c, :], start=(c == 0), stop=(c == NCH - 1))
            t_mk = p.step("pe", mmk, waits=[t_xl, t_bq] + tw + p.wr("prj1"))
            lreads.append(t_mk)
            t_ke = p.step("act", lambda e, hp=hp: e.activation(
                out=kbf[:], in_=p.prj[1][:], func=AF.Identity,
                bias=bqkc[:, 8 + hp:9 + hp]), waits=[t_mk] + p.wr("l0a_kbf"))
            p.rd("prj1", t_ke)
            t_sq1 = p.step("act", lambda e: e.activation(
                out=sqs[:], in_=qbf[:], func=AF.Square),
                waits=[t_qe] + p.wr("l0a_sqs"))
            t_n1 = p.step("pe", lambda e: e.matmul(
                ps_n[:], hsplit[:], sqs[:], start=True, stop=True),
                waits=[t_sq1] + p.wr("l0a_psn"))
            p.rd("l0a_sqs", t_n1)
            t_x1 = p.step("dve", lambda e, hp=hp: e.reduce_max(
                mq2[:, hp:hp + 1], ps_n[:], axis=AX.X),
                waits=[t_n1] + p.wr(f"l0a_mq{hp}"))
            p.rd("l0a_psn", t_x1)
            t_sq2 = p.step("act", lambda e: e.activation(
                out=sqs[:], in_=kbf[:], func=AF.Square),
                waits=[t_ke, t_x1] + p.wr("l0a_sqs"))
            t_n2 = p.step("pe", lambda e: e.matmul(
                ps_n[:], hsplit[:], sqs[:], start=True, stop=True),
                waits=[t_sq2] + p.wr("l0a_psn"))
            p.rd("l0a_sqs", t_n2)
            def kmax(e, hp=hp):
                e.reduce_max(mq2[:, 8:9], ps_n[:], axis=AX.X)
                e.drain()
                e.tensor_tensor(mq2[:, 12 + hp:13 + hp], mq2[:, 8:9],
                                mq2[:, 20 + hp:21 + hp], op=ALU.max)
                e.drain()
                e.tensor_mul(mq2[:, 10:11], mq2[:, hp:hp + 1],
                             mq2[:, 12 + hp:13 + hp])
                e.drain()
            t_x2 = p.step("dve", kmax, waits=[t_n2] + prevk)
            p.rd("l0a_psn", t_x2)
            t_sm = p.step("pe", lambda e: e.matmul(
                ps_n[0:1, 0:1], onescf[0:2, 0:1], mq2[0:2, 10:11],
                start=True, stop=True), waits=[t_x2] + tw + p.wr("l0a_psn"))
            t_s2 = p.step("dve", lambda e: e.tensor_copy(
                mq2[0:1, 11:12], ps_n[0:1, 0:1]), waits=[t_sm])
            p.rd("l0a_psn", t_s2)
            t_s3 = p.step("act", lambda e: e.activation(
                out=mq2[0:1, 11:12], in_=mq2[0:1, 11:12], func=AF.Sqrt),
                waits=[t_s2])
            t_s4 = p.step("dve", lambda e: e.tensor_scalar_mul(
                mq2[0:1, 11:12], mq2[0:1, 11:12], -1.0), waits=[t_s3])
            t_s5 = p.step("pe", lambda e, hp=hp: e.matmul(
                ps_n[0:128, 0:1] if False else p.prj[0][:, 0:1],
                cm.ones1[:], mq2[0:1, 11:12],
                start=True, stop=True),
                waits=[t_s4] + cm.ready + p.wr("prj0"))
            t_M = p.step("dve", lambda e, hp=hp: e.tensor_copy(
                negc[:, hp:hp + 1], p.prj[0][:, 0:1]),
                waits=[t_s5] + p.wr(f"l0a_negc{hp}"))
            p.rd("prj0", t_M)

            def ropeq(e):
                e.stream_shuffle(Pw[:], qbf[:], mask=shuf)
            t_r1 = p.step("dve", ropeq, waits=[t_qe] + p.wr("l0a_Pw"))
            t_r2 = p.step("pool", lambda e, ci=ci: e.tensor_mul(
                rsc2[:], qbf[:], cosb[:, ci, :]),
                waits=[t_qe, t_tab] + p.wr("l0a_rsc2"))
            p.rd("l0a_qbf", t_r2)
            t_r3 = p.step("dve", lambda e, ci=ci: e.tensor_mul(
                rsc1[:], Pw[:], sinb[:, ci, :]),
                waits=[t_r1, t_tab] + p.wr("l0a_rsc1"))
            p.rd("l0a_Pw", t_r3)
            t_r4 = p.step("pool", lambda e: e.tensor_add(
                qrope[:], rsc2[:], rsc1[:]),
                waits=[t_r2, t_r3] + p.wr("l0a_qrope"))
            p.rd("l0a_rsc1", t_r4)
            p.rd("l0a_rsc2", t_r4)
            t_r5 = p.step("dve", lambda e: e.stream_shuffle(
                Pw[:], kbf[:], mask=shuf), waits=[t_ke, t_r3] + p.wr("l0a_Pw"))
            t_r6 = p.step("pool", lambda e, ci=ci: e.tensor_mul(
                rsc2[:], kbf[:], cosb[:, ci, :]),
                waits=[t_ke, t_r4, t_tab] + p.wr("l0a_rsc2"))
            p.rd("l0a_kbf", t_r6)
            t_r7 = p.step("dve", lambda e, ci=ci: e.tensor_mul(
                rsc1[:], Pw[:], sinb[:, ci, :]),
                waits=[t_r5, t_tab] + p.wr("l0a_rsc1"))
            p.rd("l0a_Pw", t_r7)
            t_r8 = p.step("pool", lambda e, hp=hp: e.tensor_add(
                Kroll[:, hp, 1:5, :].rearrange("p s x -> p (s x)"),
                rsc2[:], rsc1[:]), waits=[t_r6, t_r7] + p.wr(f"l0a_kr{hp}"))
            p.rd("l0a_rsc1", t_r8)
            p.rd("l0a_rsc2", t_r8)
            p.rd(f"l0a_tab{ci}", t_r8)

            vevs = []
            for ts_ in range(4):
                def mmv(e, hp=hp, ts_=ts_, xl=xl):
                    for c in range(NCH):
                        e.matmul(p.prj[1][:, 0:128],
                                 xl[:, c, ts_ * 128:(ts_ + 1) * 128],
                                 wqkv[:, c, 2048 + hp * 128:2048 + (hp + 1) * 128],
                                 start=(c == 0), stop=False)
                    e.matmul(p.prj[1][:, 0:128], cm.ones1bf[:],
                             bvr0[:, hp * 128:(hp + 1) * 128],
                             start=False, stop=True)
                t_mv = p.step("pe", mmv, waits=[t_xl] + tw + p.wr("prj1"))

                def evv(e, hp=hp, ts_=ts_):
                    e.copy(out=Vroll[:, 1 + ts_, :].rearrange(
                        "p (h x) -> p h x", h=QH)[:, 2 * hp:2 * hp + 2, 0:64],
                        in_=p.prj[1][:, 0:128].rearrange("p (h x) -> p h x", h=2))
                t_ev = p.step("act", evv,
                              waits=[t_mv, t_vo] + p.wr(f"l0a_vr{1 + ts_}"))
                p.rd("prj1", t_ev)
                vevs.append(t_ev)

            for qb in range(4 if not (dbg or {}).get("no_win") else 0):
                vkr = Kroll[:].rearrange("p a b c -> p (a b c)")

                def mmsc(e, hp=hp, qb=qb, vkr=vkr):
                    for half in (0, 64):
                        if half == 64:
                            e.drain()
                        for blk, sl in ((qb, slice(0, 128)), (qb + 1, slice(128, 256))):
                            col0 = (0 if half == 0 else 256) + sl.start
                            off = hp * 640 + blk * 128
                            e.matmul(p.prj[0][:, col0:col0 + 128],
                                     vkr[half:half + 64, off:off + 128],
                                     qrope[half:half + 64,
                                           qb * 128:(qb + 1) * 128],
                                     start=True, stop=True)
                    e.drain()
                t_sc = p.step("pe", mmsc,
                              waits=[t_r4, t_r8, t_M, t_vz] + prevk +
                              p.wr("prj0"))
                p.rd("l0a_qrope", t_sc)
                p.rd(f"l0a_kr{hp}", t_sc)

                if (dbg or {}).get("plain_exp"):
                    def expw(e, hp=hp):
                        e.copy(out=Pw[:], in_=p.prj[0][:])
                else:
                    def expw(e, hp=hp):
                        e.activation(out=Pw[:], in_=p.prj[0][:], func=AF.Exp,
                                     bias=negc[:, hp:hp + 1])
                t_ex = p.step("act", expw,
                              waits=[t_sc, t_M] + p.wr("l0a_Pw"))
                p.rd("prj0", t_ex)
                p.rd(f"l0a_negc{hp}", t_ex)

                def masks(e, tt=tt, qb=qb):
                    for g in range(4):
                        mk = wmaskp if g % 2 == 0 else wmaskc
                        e.tensor_mul(Pw[:, g * 128:(g + 1) * 128],
                                     Pw[:, g * 128:(g + 1) * 128], mk[:])
                    if tt == 0 and qb == 0:
                        e.memset(Pw[:, 0:128], 0.0)
                        e.memset(Pw[:, 256:384], 0.0)
                t_mk2 = p.step("pool", masks, waits=[t_ex, t_wm]) \
                    if not (dbg or {}).get("no_mask") else t_ex

                for hh in range(2 if not (dbg or {}).get("no_av") else 0):
                    h = 2 * hp + hh

                    def mmav(e, hp=hp, qb=qb, hh=hh, h=h):
                        e.matmul(ps_av[:, 0:128],
                                 Vroll[:, qb, h * 65:h * 65 + 65],
                                 Pw[:, hh * 256:hh * 256 + 128],
                                 start=True, stop=False)
                        e.matmul(ps_av[:, 0:128],
                                 Vroll[:, qb + 1, h * 65:h * 65 + 65],
                                 Pw[:, hh * 256 + 128:hh * 256 + 256],
                                 start=False, stop=True)
                    t_av = p.step("pe", mmav,
                                  waits=[t_mk2] + vevs + prevk + p.wr("l0a_av"))
                    p.rd("l0a_Pw", t_av)
                    for sl_ in range(5):
                        p.rd(f"l0a_vr{sl_}", t_av)
                    t_rs = p.step("dve", lambda e: e.reciprocal(
                        rsw, ps_av[64:65, 0:128]),
                        waits=[t_av] + p.wr("l0a_rsw") + p.wr("l0a_rsc2"))
                    p.rd("l0a_av", t_rs)
                    t_rb = p.step("pe", lambda e: e.matmul(
                        p.prj[1][0:64, 0:128], cm.ones1[:, 0:64], rsw,
                        start=True, stop=True),
                        waits=[t_rs] + cm.ready + p.wr("prj1"))
                    p.rd("l0a_rsw", t_rb)
                    t_ae = p.step("act", lambda e: e.copy(
                        out=avw, in_=ps_av[0:64, 0:128]),
                        waits=[t_av, t_rs] + p.wr("l0a_avw"))
                    p.rd("l0a_av", t_ae)

                    def wrA(e, hp=hp, qb=qb, hh=hh, tt=tt):
                        e.tensor_mul(
                            At[hh * 64:hh * 64 + 64, hp,
                               tt * TT + qb * 128:tt * TT + (qb + 1) * 128],
                            avw, p.prj[1][0:64, 0:128])
                    t_wa = p.step("dve", wrA, waits=[t_ae, t_rb])
                    p.rd("prj1", t_wa)
                    p.rd("l0a_avw", t_wa)
                    p.rd("l0a_rsc2", t_wa)
                    lreads.append(t_wa)

        t_c1 = p.step("pool", lambda e: e.tensor_copy(
            Kroll[:, :, 0, :], Kroll[:, :, 4, :]),
            waits=[t_r8, lreads[-1]])
        t_c2 = p.step("act", lambda e: e.copy(
            out=Vroll[:, 0, :], in_=Vroll[:, 4, :]),
            waits=vevs + [t_c1] + p.wr("l0a_vr0"))
        t_c3 = p.step("dve", lambda e: e.tensor_copy(
            mq2[:, 20:28], mq2[:, 12:20]), waits=[t_x2])
        prevk = [t_c1, t_c2, t_c3]
        for hp in range(NCH):
            p.rd(f"l0a_kr{hp}", t_c1)
        p.rd("l0a_vr4", t_c2)
        for t in lreads:
            p.rd("ln_src", t)
    p.barrier()
    if dbg is not None and "At" in dbg:
        p.dma("pool", lambda e: e.dma_start(out=dbg["At"], in_=At[:]))
        p.barrier()
    if 2 not in stages:
        return

    wqa = wqkv[:, :, 0:1024]
    wka = wqkv[:, :, 1024:2048]
    t_wqa = p.dma("pool", lambda e: e.dma_start(
        out=wqa, in_=aps["L0a_WqaT"].rearrange("(c p) o -> p c o", p=128)))
    t_wka = p.dma("pool", lambda e: e.dma_start(
        out=wka, in_=aps["L0a_WkaT"].rearrange("(c p) o -> p c o", p=128)))
    bqac = p.sb("l0a_bqac", [128, NCH], F32)
    bkac = p.sb("l0a_bkac", [128, NCH], F32)
    tb2 = [p.dma("sp", lambda e: e.dma_start(out=bqac[:], in_=aps["L0a_bqac"])),
           p.dma("sp", lambda e: e.dma_start(out=bkac[:], in_=aps["L0a_bkac"]))]
    t_bq2 = p.step("dve", lambda e: e.tensor_scalar_mul(
        bqac[:], bqac[:], 0.125), waits=tb2)
    kat = Rot([wqkv[:, :, 2048:3072]])
    _vrf = Vroll[:].rearrange("p s x -> p (s x)")
    Atok = _vrf[:, 0:2080].rearrange("p (t x) -> p t x", x=65)
    _krf = Kroll[:].rearrange("p a b c -> p (a b c)")
    _krf32 = _krf.bitcast(F32)
    avt2 = _krf32[0:64, 0:TT]
    rsum2 = _krf32[0:1, TT:2 * TT]
    obf2 = _krf[0:64, 4 * TT:5 * TT]
    qat = qrope
    Pb2 = [Pw, qbf]
    pbq = p.psum("l0a_pbq", [128, 1024], BF16)
    id64d = p.sb("l0a_id64d", [128, 64], BF16)
    t_id64 = p.dma("pool", lambda e: e.dma_start(out=id64d[:], in_=aps["ident64d"]))
    t_ao = p.step("pool", lambda e: e.memset(
        Atok[:, :, 64:65], 1.0), waits=p.wr("l0a_vr_all"))

    for h in range(QH):
        qc = h // 2
        qr = (h % 2) * 64
        kb, kbw, kbi = kat.acquire()
        kevs2 = []
        knsteps = []
        for ktile in range(NTT):
            def mmka(e, h=h, ktile=ktile):
                for c in range(NCH):
                    e.matmul(p.prj[0][0:64, :], wka[:, c, h * 64:(h + 1) * 64],
                             At[:, c, ktile * TT:(ktile + 1) * TT],
                             start=(c == 0), stop=(c == NCH - 1))
            t_km = p.step("pe", mmka,
                          waits=[t_wka] + kbw + p.wr("prj0"))
            kbw = []
            t_ke = p.step("act", lambda e, ktile=ktile, qr=qr, kb=kb, qc=qc:
                          e.activation(out=kb[qr:qr + 64, ktile // 2,
                                              (ktile % 2) * TT:
                                              (ktile % 2) * TT + TT],
                                       in_=p.prj[0][0:64, :],
                                       func=AF.Identity,
                                       bias=bkac[qr:qr + 64, qc:qc + 1]),
                          waits=[t_km] + tb2)
            p.rd("prj0", t_ke)
            kevs2.append(t_ke)
            t_s = p.step("act", lambda e, kb=kb, ktile=ktile: e.activation(
                out=sqs[:], in_=kb[:, ktile // 2, (ktile % 2) * TT:
                                   (ktile % 2) * TT + TT],
                func=AF.Square), waits=[t_ke] + p.wr("l0a_sqs"))
            t_n = p.step("pe", lambda e, qr=qr: e.matmul(
                ps_n[0:1, :], hsplit[:, qr // 64:qr // 64 + 1], sqs[:],
                start=True, stop=True), waits=[t_s] + p.wr("l0a_psn"))
            p.rd("l0a_sqs", t_n)

            def knm(e, ktile=ktile):
                e.reduce_max(mq2[0:1, 8:9], ps_n[0:1, :], axis=AX.X)
                e.drain()
                if ktile == 0:
                    e.tensor_copy(mq2[0:1, 9:10], mq2[0:1, 8:9])
                else:
                    e.tensor_tensor(mq2[0:1, 9:10], mq2[0:1, 9:10],
                                    mq2[0:1, 8:9], op=ALU.max)
                e.drain()
            t_kn = p.step("dve", knm, waits=[t_n] + knsteps)
            knsteps = [t_kn]
            p.rd("l0a_psn", t_kn)
        aevs = []
        for kt in range(NKT):
            t_tr = p.step("pe", lambda e, kt=kt, qr=qr, qc=qc: e.matmul(
                pbq[:, 0:64], At[qr:qr + 64, qc, kt * 128:(kt + 1) * 128],
                id64d[qr:qr + 64, :], start=True, stop=True,
                is_transpose=True), waits=[t_id64] + p.wr("l0a_pbq"))
            t_te = p.step("act", lambda e, kt=kt: e.copy(
                out=Atok[:, kt, 0:64], in_=pbq[:, 0:64]),
                waits=[t_tr, t_ao] + p.wr(f"l0a_at{kt % 4}"))
            p.rd("l0a_pbq", t_te)
            aevs.append(t_te)

        for qs in range(NTT):
            def mmqa(e, h=h, qs=qs):
                for c in range(NCH):
                    e.matmul(p.prj[0][0:64, :], wqa[:, c, h * 64:(h + 1) * 64],
                             At[:, c, qs * TT:(qs + 1) * TT],
                             start=(c == 0), stop=(c == NCH - 1))
            t_qm = p.step("pe", mmqa, waits=[t_wqa, t_bq2] + p.wr("prj0"))
            t_qe = p.step("act", lambda e, qr=qr, qc=qc: e.activation(
                out=qat[qr:qr + 64, :], in_=p.prj[0][0:64, :],
                func=AF.Identity, scale=0.125,
                bias=bqac[qr:qr + 64, qc:qc + 1]),
                waits=[t_qm] + p.wr("l0a_qat"))
            p.rd("prj0", t_qe)
            t_s = p.step("act", lambda e: e.activation(
                out=sqs[:], in_=qat[:], func=AF.Square),
                waits=[t_qe] + p.wr("l0a_sqs"))
            t_n = p.step("pe", lambda e, qr=qr: e.matmul(
                ps_n[0:1, :], hsplit[:, qr // 64:qr // 64 + 1], sqs[:],
                start=True, stop=True), waits=[t_s] + p.wr("l0a_psn"))
            p.rd("l0a_sqs", t_n)

            def mkM(e):
                e.reduce_max(mq2[0:1, 10:11], ps_n[0:1, :], axis=AX.X)
                e.drain()
                e.tensor_mul(mq2[0:1, 10:11], mq2[0:1, 10:11], mq2[0:1, 9:10])
                e.drain()
            t_m1 = p.step("dve", mkM, waits=[t_n] + knsteps + p.wr("l0a_m2"))
            p.rd("l0a_psn", t_m1)
            t_m2 = p.step("act", lambda e: e.activation(
                out=mq2[0:1, 11:12], in_=mq2[0:1, 10:11], func=AF.Sqrt),
                waits=[t_m1])
            t_m3 = p.step("dve", lambda e: e.tensor_scalar_mul(
                mq2[0:1, 11:12], mq2[0:1, 11:12], -1.0), waits=[t_m2])
            p.rd("l0a_m2", t_m3)
            t_m4 = p.step("pe", lambda e: e.matmul(
                p.prj[0][:, 0:1], cm.ones1[:], mq2[0:1, 11:12],
                start=True, stop=True),
                waits=[t_m3] + cm.ready + p.wr("prj0"))
            t_M = p.step("dve", lambda e: e.tensor_copy(
                negc[:, 0:1], p.prj[0][:, 0:1]),
                waits=[t_m4] + p.wr("l0a_negc0"))
            p.rd("prj0", t_M)

            nkt = 4 * qs + 4
            t_exps = []
            for kt in range(nkt):
                pb = p.prj[kt % 2]
                pt = Pb2[kt % 2]

                def mmsc(e, pb=pb, kt=kt, qr=qr, kb=kb):
                    e.matmul(pb[:],
                             kb[qr:qr + 64, kt // 8, (kt % 8) * 128:
                                (kt % 8) * 128 + 128],
                             qat[qr:qr + 64, :], start=True, stop=True)
                t_sc = p.step("pe", mmsc,
                              waits=[t_qe, t_M, kevs2[-1]] +
                              p.wr(f"prj{kt % 2}"))
                kat.read(kbi, t_sc)
                p.rd("l0a_qat", t_sc)

                def expev(e, pb=pb, pt=pt):
                    e.activation(out=pt[:], in_=pb[:], func=AF.Exp,
                                 bias=negc[:, 0:1])
                t_ex = p.step("act", expev,
                              waits=[t_sc, t_M] + p.wr(f"l0a_P2{kt % 2}"))
                p.rd(f"prj{kt % 2}", t_ex)
                p.rd("l0a_negc0", t_ex)
                t_last = t_ex
                if kt >= 4 * qs:
                    def msk(e, kt=kt, qs=qs, pt=pt):
                        e.affine_select(pt[:], pt[:], pattern=[[1, TT]],
                                        compare_op=ALU.is_ge, fill=0.0,
                                        base=qs * TT - kt * 128,
                                        channel_multiplier=-1)
                    t_last = p.step("pool", msk, waits=[t_ex])

                def mmav(e, pt=pt, kt=kt, nkt=nkt):
                    e.matmul(ps_av[:], Atok[:, kt, :], pt[:],
                             start=(kt == 0), stop=(kt == nkt - 1))
                t_av = p.step("pe", mmav, waits=[t_last, aevs[-1]] +
                              (p.wr("l0a_av") if kt == 0 else []))
                p.rd(f"l0a_P2{kt % 2}", t_av)
                p.rd(f"l0a_at{kt % 4}", t_av)
                t_exps.append(t_av)

            t_rs = p.step("dve", lambda e: e.reciprocal(
                rsum2, ps_av[64:65, :]), waits=[t_exps[-1]] + p.wr("l0a_rs2"))
            p.rd("l0a_av", t_rs)
            t_rb = p.step("pe", lambda e: e.matmul(
                p.prj[0][0:64, :], cm.ones1[:, 0:64], rsum2,
                start=True, stop=True),
                waits=[t_rs] + cm.ready + p.wr("prj0"))
            p.rd("l0a_rs2", t_rb)
            t_ae = p.step("act", lambda e: e.copy(out=avt2, in_=ps_av[0:64, :]),
                          waits=[t_exps[-1]] + p.wr("l0a_avt2"))
            p.rd("l0a_av", t_ae)
            t_ob = p.step("dve", lambda e: e.tensor_mul(
                obf2, avt2, p.prj[0][0:64, :]),
                waits=[t_ae, t_rb] + p.wr("l0a_obf2"))
            p.rd("prj0", t_ob)
            p.rd("l0a_avt2", t_ob)
            t_st = p.dma("act", lambda e, h=h, qs=qs: e.dma_start(
                out=attnout[h * 64:(h + 1) * 64, qs * TT:(qs + 1) * TT],
                in_=obf2), waits=[t_ob])
            p.rd("l0a_obf2", t_st)

    p.barrier()
    wd = wqkv[:, :, 2048:3072]
    t_wd = p.dma("pool", lambda e: e.dma_start(
        out=wd, in_=aps["L0a_WdT"].rearrange("(c p) o -> p c o", p=128)))
    bdc = p.sb("l0a_bdc", [128, NCH], F32)
    t_bd = p.dma("sp", lambda e: e.dma_start(out=bdc[:], in_=aps["L0a_bdc"]))
    _atf = At[:].rearrange("p c t -> p (c t)")
    ytile = _atf.bitcast(F32)[:, 0:4096].rearrange("p (c t) -> p c t", c=NCH)
    aot = Rot([_atf[:, 8192 + i * 4096:8192 + (i + 1) * 4096]
               .rearrange("p (c t) -> p c t", c=NCH) for i in range(2)])
    for tt in range(NTT):
        at, aw, ai = aot.acquire()
        t_ld = p.dma("sp", lambda e, tt=tt, at=at: e.dma_start(
            out=at, in_=x_ap(attnout, tt * TT, TT)), waits=aw)
        evs = []
        for oc in range(NCH):
            pb = p.prj[oc % 2]

            def mmo(e, oc=oc, pb=pb, at=at):
                for c in range(NCH):
                    e.matmul(pb[:], wd[:, c, oc * 128:(oc + 1) * 128],
                             at[:, c, :], start=(c == 0), stop=(c == NCH - 1))
            t_mm = p.step("pe", mmo,
                          waits=[t_ld, t_wd] + p.wr(f"prj{oc % 2}"))

            def evo(e, oc=oc, pb=pb):
                e.activation(out=ytile[:, oc, :], in_=pb[:], func=AF.Identity,
                             bias=bdc[:, oc:oc + 1])
            t_ev = p.step("act", evo,
                          waits=[t_mm, t_bd] + p.wr(f"l0a_yt{oc}"))
            p.rd(f"prj{oc % 2}", t_ev)
            evs.append(t_ev)
        aot.read(ai, *evs)
        t_stx, ycons = emit_resid_out_tile(p, cm, ytile, evs[-1], x_in, x_out,
                                           tt)
        for oc in range(NCH):
            for t in ycons:
                p.rd(f"l0a_yt{oc}", t)
    p.barrier()


def host_prep(inputs):
    f32 = np.float32
    C = {}
    C["ones1"] = np.ones((1, 128), f32)
    C["onescol"] = np.ones((128, 1), f32)
    C["ident128"] = np.eye(128, dtype=f32)
    C["ident64d"] = np.ascontiguousarray(
        np.vstack([np.eye(64, dtype=f32), np.eye(64, dtype=f32)]))
    C["ident8"] = np.eye(8, dtype=f32)
    C["ident16"] = np.eye(16, dtype=f32)
    C["ident64"] = np.eye(64, dtype=f32)
    C["rep8"] = np.repeat(np.eye(8, dtype=f32), 16, axis=1)
    C["rep16"] = np.repeat(np.eye(16, dtype=f32), 8, axis=1)
    hs = np.zeros((128, 2), f32)
    hs[0:64, 0] = 1.0
    hs[64:128, 1] = 1.0
    C["hsplit"] = hs
    gp = np.zeros((8, 2), f32)
    gp[0:4, 0] = 1.0
    gp[4:8, 1] = 1.0
    C["grp8"] = gp
    NB, MD = 32, 128
    u = np.arange(8704)
    rel = u - 128 - 4095
    valid = (u >= 128) & (u <= 128 + 8190)
    nb = NB // 2
    buckets = (rel > 0).astype(np.int64) * nb
    rp = np.abs(rel)
    mx = nb // 2
    is_small = rp < mx
    with np.errstate(divide="ignore"):
        large = mx + (np.log(np.maximum(rp, 1) / mx) /
                      np.log(MD / mx) * (nb - mx)).astype(np.int64)
    large = np.minimum(large, nb - 1)
    bkt = buckets + np.where(is_small, rp, large)
    oh = np.zeros((NB, 8704), f32)
    oh[bkt[valid], u[valid]] = 1.0
    C["OH"] = oh

    def col128(v):
        k = v.shape[0] // 128
        return np.ascontiguousarray(v.reshape(k, 128).T)

    W = {}
    W["L0f_basewT"] = np.ascontiguousarray(inputs["L0_ffn_base_w"].T)
    W["L0f_baseb"] = col128(inputs["L0_ffn_base_b"])
    W["L0f_A"] = np.ascontiguousarray(
        inputs["L0_ffn_lora_A"].transpose(1, 0, 2).reshape(D, 128))
    W["L0f_B"] = np.ascontiguousarray(inputs["L0_ffn_lora_B"].reshape(128, D))
    W["L0f_rwT"] = np.ascontiguousarray(inputs["L0_ffn_router_w"].T)
    W["L0f_rb"] = inputs["L0_ffn_router_b"].reshape(8, 1)

    def b1cols(b):
        E_, F_ = b.shape
        return np.ascontiguousarray(b.reshape(E_, F_ // 128, 128).transpose(0, 2, 1))

    W["L1f_g1c"] = col128(inputs["L1_ffn_g1"])
    W["L1f_be1c"] = col128(inputs["L1_ffn_be1"])
    W["L1f_g2c"] = col128(inputs["L1_ffn_g2"])
    W["L1f_be2c"] = col128(inputs["L1_ffn_be2"])
    W["L1f_mixw"] = np.ascontiguousarray(inputs["L1_ffn_mix_w"])
    W["L1f_mixbc"] = col128(inputs["L1_ffn_mix_b"])
    W["L1f_rw"] = np.ascontiguousarray(inputs["L1_ffn_router_w"])
    W["L1f_w1"] = np.ascontiguousarray(inputs["L1_ffn_w1"])
    W["L1f_b1c"] = b1cols(inputs["L1_ffn_b1"])
    W["L1f_w2"] = np.ascontiguousarray(inputs["L1_ffn_w2"])
    W["L1f_b2"] = np.ascontiguousarray(inputs["L1_ffn_b2"])
    W["L2f_gate"] = np.ascontiguousarray(inputs["L2_ffn_gate"])
    W["L2f_W1"] = np.ascontiguousarray(inputs["L2_ffn_W1"])
    W["L2f_b1c"] = b1cols(inputs["L2_ffn_b1"])
    W["L2f_W2"] = np.ascontiguousarray(inputs["L2_ffn_W2"])
    W["L2f_b2"] = np.ascontiguousarray(inputs["L2_ffn_b2"])
    W["L3f_rwT"] = np.ascontiguousarray(inputs["L3_ffn_router_w"].T)
    W["L3f_Wg"] = np.ascontiguousarray(inputs["L3_ffn_Wg"])
    W["L3f_Wu"] = np.ascontiguousarray(inputs["L3_ffn_Wu"])
    W["L3f_Wd"] = np.ascontiguousarray(inputs["L3_ffn_Wd"])
    W["L3f_Sg"] = np.ascontiguousarray(inputs["L3_ffn_Sg"])
    W["L3f_Su"] = np.ascontiguousarray(inputs["L3_ffn_Su"])
    W["L3f_Sd"] = np.ascontiguousarray(inputs["L3_ffn_Sd"])
    W["L3f_sgw"] = col128(inputs["L3_ffn_shared_gate_w"][:, 0])
    W["L2a_wqT"] = np.ascontiguousarray(inputs["L2_attn_wq"].T)
    W["L2a_wkT"] = np.ascontiguousarray(inputs["L2_attn_wk"].T)
    W["L2a_wvT"] = np.ascontiguousarray(inputs["L2_attn_wv"].T)
    W["L2a_woT"] = np.ascontiguousarray(inputs["L2_attn_wo"].T)
    W["L2a_bqc"] = col128(inputs["L2_attn_bq"])
    W["L2a_bkc"] = col128(inputs["L2_attn_bk"])
    W["L2a_bv"] = inputs["L2_attn_bv"]
    W["L2a_boc"] = col128(inputs["L2_attn_bo"])
    W["L3a_Wq"] = np.ascontiguousarray(inputs["L3_attn_Wq"])
    W["L3a_Wk"] = np.ascontiguousarray(inputs["L3_attn_Wk"])
    W["L3a_Wv"] = np.ascontiguousarray(inputs["L3_attn_Wv"])
    W["L3a_Wo"] = np.ascontiguousarray(inputs["L3_attn_Wo"])
    W["L3a_bqc"] = col128(inputs["L3_attn_bq"])
    W["L3a_bkc"] = col128(inputs["L3_attn_bk"])
    W["L3a_bv"] = inputs["L3_attn_bv"]
    W["L3a_boc"] = col128(inputs["L3_attn_bo"])
    W["L3a_gruw2"] = np.ascontiguousarray(
        np.vstack([inputs["L3_attn_gru_w"], inputs["L3_attn_gru_w"]]))
    W["L3a_grub"] = inputs["L3_attn_gru_b"].reshape(8, 1)
    W["L3a_gruc"] = inputs["L3_attn_gru_const"].reshape(1, 16)
    W["L3a_rel"] = np.ascontiguousarray(inputs["L3_attn_rel_embed"])
    W["L1a_Win"] = np.ascontiguousarray(inputs["L1_attn_W_in"])
    W["L1a_binc"] = col128(inputs["L1_attn_b_in"])
    W["L1a_Wxp"] = np.ascontiguousarray(inputs["L1_attn_W_xproj"])
    W["L1a_bxB"] = inputs["L1_attn_b_xproj"][0:16].reshape(16, 1)
    W["L1a_bxC"] = inputs["L1_attn_b_xproj"][16:32].reshape(16, 1)
    W["L1a_Wdt"] = np.ascontiguousarray(inputs["L1_attn_W_dt"])
    W["L1a_bdtc"] = col128(inputs["L1_attn_b_dt"])
    W["L1a_Wout"] = np.ascontiguousarray(inputs["L1_attn_W_out"])
    W["L1a_boutc"] = col128(inputs["L1_attn_b_out"])
    W["L1a_Alog"] = np.ascontiguousarray(
        inputs["L1_attn_A_log"].reshape(16, 128, 16).transpose(1, 0, 2)
        .reshape(128, 256))
    W["L1a_Dcol"] = col128(inputs["L1_attn_D_param"])
    W["L0a_WqkvT"] = np.ascontiguousarray(inputs["L0_attn_Wqkv"].T)
    W["L0a_bqkvc"] = np.ascontiguousarray(
        inputs["L0_attn_bqkv"].reshape(24, 128).T)
    W["L0a_WqaT"] = np.ascontiguousarray(inputs["L0_attn_Wqa"].T)
    W["L0a_bqac"] = col128(inputs["L0_attn_bqa"])
    W["L0a_WkaT"] = np.ascontiguousarray(inputs["L0_attn_Wka"].T)
    W["L0a_bkac"] = col128(inputs["L0_attn_bka"])
    W["L0a_WdT"] = np.ascontiguousarray(inputs["L0_attn_Wd"].T)
    W["L0a_bdc"] = col128(inputs["L0_attn_bd"])
    W["L0a_bv"] = inputs["L0_attn_bqkv"][2048:3072]
    RD = 16
    invf = 1.0 / (10000.0 ** (np.arange(0, RD, 2, dtype=np.float64) / RD))
    t_ = np.arange(S, dtype=np.float64)
    ang = np.outer(invf, t_)
    cosr = np.ones((128, S), np.float64)
    sinr = np.zeros((128, S), np.float64)
    for blk in range(2):
        base = blk * 64
        cosr[base:base + 8] = np.cos(ang)
        cosr[base + 8:base + 16] = np.cos(ang)
        sinr[base:base + 8] = -np.sin(ang)
        sinr[base + 8:base + 16] = np.sin(ang)
    W["L0a_cos"] = cosr.astype(f32)
    W["L0a_sin"] = sinr.astype(f32)
    kp_ = np.arange(128)[:, None]
    qf_ = np.arange(128)[None, :]
    W["L0a_wmp"] = ((qf_ - kp_ + 128 >= 0) & (qf_ - kp_ + 128 <= 15)).astype(f32)
    W["L0a_wmc"] = ((qf_ - kp_ >= 0) & (qf_ - kp_ <= 15)).astype(f32)
    return C, W


BLOCK_EMITTERS = {0: emit_L0_attn, 1: emit_L0_ffn, 2: emit_L1_attn,
                  3: emit_L1_ffn, 4: emit_L2_attn, 5: emit_L2_ffn,
                  6: emit_L3_attn, 7: emit_L3_ffn}
BLOCK_WEIGHT_PREFIX = {0: ("L0a_",), 1: ("L0f_",), 2: ("L1a_",),
                       3: ("L1f_",), 4: ("L2a_",), 5: ("L2f_",),
                       6: ("L3a_",), 7: ("L3f_",)}


def declare_dram_inputs(nc, arrs):
    aps = {}
    for name, arr in arrs.items():
        aps[name] = nc.dram_tensor("in_" + name, list(arr.shape), F32,
                                   kind="ExternalInput").ap()
    return aps


def build_block_launch(bidx, C, W):
    nc = bass.Bass("TRN2", target_bir_lowering=False)
    x_in = nc.dram_tensor("x_in", [D, S], F32, kind="ExternalInput").ap()
    x_out = nc.dram_tensor("x_out", [D, S], F32, kind="ExternalOutput").ap()
    arrs = dict(C)
    pref = BLOCK_WEIGHT_PREFIX[bidx]
    for k, v in W.items():
        if any(k.startswith(px) for px in pref):
            arrs[k] = v
    aps = declare_dram_inputs(nc, arrs)
    p = Prog(nc)
    p.aps = aps
    cm = Common(p, aps)
    if bidx == 3:
        xmid_dram = nc.dram_tensor("l1f_xmid_dram", [D, S], F32,
                                   kind="Internal").ap()
        BLOCK_EMITTERS[bidx](p, cm, aps, x_in, x_out, xmid_dram)
    else:
        BLOCK_EMITTERS[bidx](p, cm, aps, x_in, x_out)
    p.emit()
    return nc, set(arrs.keys())


def build_fused_launch(C, W):
    nc = bass.Bass("TRN2", target_bir_lowering=False)
    x_in = nc.dram_tensor("x_in", [D, S], F32, kind="ExternalInput").ap()
    x_out = nc.dram_tensor("x_out", [D, S], F32, kind="ExternalOutput").ap()
    xa = nc.dram_tensor("x_ping", [D, S], F32, kind="Internal").ap()
    xb = nc.dram_tensor("x_pong", [D, S], F32, kind="Internal").ap()
    arrs = {**C, **W}
    aps = declare_dram_inputs(nc, arrs)
    p = Prog(nc)
    p.aps = aps
    cm = Common(p, aps)
    bufs = [x_in, xa, xb, xa, xb, xa, xb, xa, x_out]
    for bidx in range(8):
        p.push_scope()
        if bidx == 3:
            xmid = nc.dram_tensor("l1f_xmid_dram", [D, S], F32,
                                  kind="Internal").ap()
            BLOCK_EMITTERS[bidx](p, cm, aps, bufs[bidx], bufs[bidx + 1], xmid)
        else:
            BLOCK_EMITTERS[bidx](p, cm, aps, bufs[bidx], bufs[bidx + 1])
        p.barrier()
        p.pop_scope()
    p.emit()
    return nc, set(arrs.keys())


def run_fused(x_bt, C, W):
    nc, names = build_fused_launch(C, W)
    allarr = {**C, **W}
    in_maps = []
    for c in range(8):
        m = {"x_in": np.ascontiguousarray(x_bt[c % 4])}
        for name in names:
            m["in_" + name] = allarr[name]
        in_maps.append(m)
    res = run_bass_kernel_spmd(nc, in_maps, core_ids=list(range(8)))
    return np.stack([res.results[c]["x_out"] for c in range(4)])


def run_block_cw(bidx, x_bt, C, W):
    nc, names = build_block_launch(bidx, C, W)
    allarr = {**C, **W}
    in_maps = []
    for c in range(8):
        m = {"x_in": np.ascontiguousarray(x_bt[c % 4])}
        for name in names:
            m["in_" + name] = allarr[name]
        in_maps.append(m)
    res = run_bass_kernel_spmd(nc, in_maps, core_ids=list(range(8)))
    return np.stack([res.results[c]["x_out"] for c in range(4)])


def run_block(bidx, x_bt, inputs):
    C, W = host_prep(inputs)
    return run_block_cw(bidx, x_bt, C, W)


def kernel(**inputs):
    inp = {k: np.asarray(v) for k, v in inputs.items()}
    x = inp["L0_attn_hidden_states"].astype(np.float32)
    x_bt = np.ascontiguousarray(x.transpose(0, 2, 1))
    C, W = host_prep(inp)
    x_bt = run_fused(x_bt, C, W)
    return np.ascontiguousarray(x_bt.transpose(0, 2, 1)).astype(np.float32)

